# version 14
# baseline (speedup 1.0000x reference)
import numpy as np
from contextlib import ExitStack

import concourse.bass as bass
import concourse.mybir as mybir
from concourse.bass_utils import run_bass_kernel_spmd

F32 = mybir.dt.float32
BF16 = mybir.dt.bfloat16
AF = mybir.ActivationFunctionType
ALU = mybir.AluOpType
AX = mybir.AxisListType

D_MODEL = 1024
D_INNER = 2048
E2 = 4096
NST = 16
NLAYER = 8
NEXP = 16
EPS = 1e-5

DBLK = D_MODEL // 128
CBLK = D_INNER // 128
EBLK = E2 // 128
FD = 512
FMAX = 3.4028234663852886e38


def _tile_lhsT(wT):
    K, M = wT.shape
    return np.ascontiguousarray(
        wT.reshape(K // 128, 128, M // 128, 128).transpose(2, 0, 1, 3))


def prepare_host_inputs(ins):
    f = np.float32
    o = {}
    x = np.asarray(ins["L0_attn_x"], f)
    o["xT"] = np.ascontiguousarray(x.transpose(0, 2, 1))

    wi = np.asarray(ins["L0_attn_W_in"], f)
    o["wi_t"] = np.stack([_tile_lhsT(np.ascontiguousarray(wi[l].T)) for l in range(NLAYER)])
    wo = np.asarray(ins["L0_attn_W_out"], f)
    o["wo_t"] = np.stack([_tile_lhsT(np.ascontiguousarray(wo[l].T)) for l in range(NLAYER)])
    wx = np.asarray(ins["L0_attn_W_x"], f)
    o["wx_t"] = np.ascontiguousarray(
        wx.transpose(0, 2, 1).reshape(NLAYER, CBLK, 128, 96))
    wdt = np.asarray(ins["L0_attn_W_dt"], f)
    bdt = np.asarray(ins["L0_attn_b_dt"], f)
    rows = np.concatenate([wdt.transpose(0, 2, 1), bdt[:, None, :]], axis=1)
    o["wdt_t"] = np.ascontiguousarray(
        rows.reshape(NLAYER, 65, CBLK, 128).transpose(0, 2, 1, 3))
    o["conv_w"] = np.ascontiguousarray(
        np.asarray(ins["L0_attn_conv_w"], f).reshape(NLAYER, CBLK, 128, 4).transpose(0, 2, 1, 3))
    o["conv_b"] = np.ascontiguousarray(
        np.asarray(ins["L0_attn_conv_b"], f).reshape(NLAYER, CBLK, 128).transpose(0, 2, 1))
    o["a_log"] = np.ascontiguousarray(
        np.asarray(ins["L0_attn_A_log"], f).reshape(NLAYER, CBLK, 128, NST).transpose(0, 2, 1, 3))
    o["d_par"] = np.ascontiguousarray(
        np.asarray(ins["L0_attn_D_par"], f).reshape(NLAYER, CBLK, 128).transpose(0, 2, 1))

    up = np.asarray(ins["L0_ffn_up_proj"], f)
    o["up_t"] = np.stack([_tile_lhsT(up[k]) for k in range(NEXP)])
    gp = np.asarray(ins["L0_ffn_gate_proj"], f)
    o["gp_t"] = np.stack([_tile_lhsT(gp[k]) for k in range(NEXP)])
    dn = np.asarray(ins["L0_ffn_down_proj"], f)
    o["dn_t"] = np.stack([_tile_lhsT(dn[k]) for k in range(NEXP)])
    o["shg_t"] = _tile_lhsT(np.ascontiguousarray(np.asarray(ins["L0_ffn_sh_gate"], f).T))
    o["shu_t"] = _tile_lhsT(np.ascontiguousarray(np.asarray(ins["L0_ffn_sh_up"], f).T))
    shd_full = _tile_lhsT(np.ascontiguousarray(np.asarray(ins["L0_ffn_sh_down"], f).T))
    o["shd_t"] = np.ascontiguousarray(np.stack([shd_full[:, 0:4], shd_full[:, 4:8]]))
    rw = np.asarray(ins["L0_ffn_router_w"], f)
    o["rw_t"] = np.ascontiguousarray(rw.T.reshape(DBLK, 128, NEXP))
    o["rb"] = np.ascontiguousarray(np.asarray(ins["L0_ffn_router_bias"], f).reshape(NEXP, 1))

    o["ones128"] = np.ones((128, 128), f)
    o["sel16"] = np.ascontiguousarray(np.kron(np.eye(16, dtype=f), np.ones((1, 128), f)))
    o["ident128"] = np.eye(128, dtype=f)
    return o


class Buf:
    __slots__ = ("w", "readers", "name")

    def __init__(self, name):
        self.name = name
        self.w = None
        self.readers = []


class Prog:
    ENGINES = ("sync", "gpsimd", "tensor", "vector", "scalar")
    DEF_SEM = dict(tensor="pe", vector="dve", scalar="act", gpsimd="pool")
    NLANE = 4

    def __init__(self, nc):
        self.nc = nc
        self.q = {e: [] for e in self.ENGINES}
        self.stack = ExitStack()
        self.sems = {}
        self.cnt = {}
        self.hiwater = {e: {} for e in self.ENGINES}
        names = ["pe", "act", "dve", "pool", "st"]
        self.dma_lanes = {}
        for qn, tag in (("sync", "s"), ("gpsimd", "g"), ("scalar", "c")):
            lanes = [f"dma_{tag}{i}" for i in range(self.NLANE)]
            self.dma_lanes[qn] = lanes
            names += lanes
        for s in names:
            self.sems[s] = self.stack.enter_context(nc.semaphore(s))
            self.cnt[s] = 0
        self.dma_rot = {qn: 0 for qn in self.dma_lanes}

    _serial = 0

    def _uniq(self, name):
        Prog._serial += 1
        return f"{name}_{Prog._serial}"

    def scope(self):
        return ExitStack()

    def sb(self, scope, name, shape, dt):
        name = self._uniq(name)
        t = (scope or self.stack).enter_context(self.nc.sbuf_tensor(name, shape, dt))
        return t, Buf(name)

    def ps(self, scope, name, shape):
        name = self._uniq(name)
        t = (scope or self.stack).enter_context(self.nc.psum_tensor(name, shape, F32))
        return t, Buf(name)

    def _wait(self, eng, sem, val):
        if val <= 0:
            return
        hw = self.hiwater[eng]
        if hw.get(sem, 0) >= val:
            return
        hw[sem] = val
        h = self.sems[sem]
        self.q[eng].append(lambda e, h=h, v=val: e.wait_ge(h, v))

    def _deps(self, eng, reads, writes):
        for b in reads:
            if b.w:
                self._wait(eng, b.w[0], b.w[1])
        for b in writes:
            if b.w:
                self._wait(eng, b.w[0], b.w[1])
            for (s, v, src) in b.readers:
                self._wait(eng, s, v)

    def _commit(self, eng, sem, val, reads, writes):
        for b in reads:
            b.readers.append((sem, val, eng))
        for b in writes:
            b.w = (sem, val, eng)
            b.readers = []

    def op(self, eng, fn, reads=(), writes=()):
        self._deps(eng, reads, writes)
        sem = self.DEF_SEM[eng]
        self.cnt[sem] += 1
        v = self.cnt[sem]
        h = self.sems[sem]
        self.q[eng].append(lambda e, fn=fn, h=h: fn(e).then_inc(h, 1))
        self._commit(eng, sem, v, reads, writes)
        return v

    def mmg(self, fns, reads=(), writes=()):
        self._deps("tensor", reads, writes)
        self.cnt["pe"] += 1
        v = self.cnt["pe"]
        h = self.sems["pe"]
        for i, fn in enumerate(fns):
            if i == len(fns) - 1:
                self.q["tensor"].append(lambda e, fn=fn, h=h: fn(e).then_inc(h, 1))
            else:
                self.q["tensor"].append(fn)
        self._commit("tensor", "pe", v, reads, writes)
        return v

    def dma(self, queue, out_ap, in_ap, reads=(), writes=(), sem=None, **kw):
        self._deps(queue, reads, writes)
        if sem is None:
            lanes = self.dma_lanes[queue]
            s = lanes[self.dma_rot[queue] % self.NLANE]
            self.dma_rot[queue] += 1
        else:
            s = sem
        self._wait(queue, s, self.cnt[s])
        self.cnt[s] += 16
        v = self.cnt[s]
        h = self.sems[s]
        self.q[queue].append(
            lambda e, o=out_ap, i=in_ap, h=h, kw=kw: e.dma_start(
                out=(o() if callable(o) else o), in_=(i() if callable(i) else i),
                **kw).then_inc(h, 16))
        self._commit(queue, s, v, reads, writes)
        return v

    def barrier(self):
        sems = ["dve", "act", "pe", "pool"]
        for lanes in self.dma_lanes.values():
            sems += lanes
        for eng in self.ENGINES:
            for s in sems:
                self._wait(eng, s, self.cnt[s])

    def finalize(self):
        nc = self.nc
        blk_cm = nc.Block()
        block = blk_cm.__enter__()
        for eng in self.ENGINES:
            closures = self.q[eng]
            getattr(block, eng)(lambda e, cs=closures: [fn(e) for fn in cs] and None)
        blk_cm.__exit__(None, None, None)
        self.stack.close()


def build(tok=4096, th=1024, layers=NLAYER, ffn=True, dumps=(), stop_after=99, ph4_mode=0):
    TOK, TH, NL = tok, th, layers
    NFD = TOK // FD
    NTH = TOK // TH
    HALF = TOK // 2
    HTOK = TOK // 2
    NFD_F = HTOK // FD

    nc = bass.Bass("TRN2", target_bir_lowering=False)

    def din(name, shape):
        return nc.dram_tensor(name, list(shape), F32, kind="ExternalInput").ap()

    xT = din("xT", (D_MODEL, TOK))
    wi_t = din("wi_t", (NLAYER, EBLK, DBLK, 128, 128))
    wo_t = din("wo_t", (NLAYER, DBLK, CBLK, 128, 128))
    wx_t = din("wx_t", (NLAYER, CBLK, 128, 96))
    wdt_t = din("wdt_t", (NLAYER, CBLK, 65, 128))
    conv_w = din("conv_w", (NLAYER, 128, CBLK, 4))
    conv_b = din("conv_b", (NLAYER, 128, CBLK))
    a_log = din("a_log", (NLAYER, 128, CBLK, NST))
    d_par = din("d_par", (NLAYER, 128, CBLK))
    up_t = din("up_t", (NEXP, 4, DBLK, 128, 128))
    gp_t = din("gp_t", (NEXP, 4, DBLK, 128, 128))
    dn_t = din("dn_t", (NEXP, DBLK, 4, 128, 128))
    shg_t = din("shg_t", (DBLK, DBLK, 128, 128))
    shu_t = din("shu_t", (DBLK, DBLK, 128, 128))
    shd_t = din("shd_t", (2, DBLK, 4, 128, 128))
    rw_t = din("rw_t", (DBLK, 128, NEXP))
    rb_in = din("rb", (NEXP, 1))
    ones_in = din("ones128", (128, 128))
    sel16_in = din("sel16", (NST, NST * 128))
    tok_base = nc.dram_tensor("tok_base", [1, 1], mybir.dt.int32, kind="ExternalInput").ap()
    ident_in = din("ident128", (128, 128))

    outT = nc.dram_tensor("outT", [D_MODEL, HTOK], F32, kind="ExternalOutput").ap()

    def dscr(name, shape, dt):
        return nc.dram_tensor(name, list(shape), dt).ap()

    x_str = dscr("x_str", (DBLK, 128, TOK), F32)
    u_scr = dscr("u_scr", (CBLK, 128, TOK), BF16)
    sz_scr = dscr("sz_scr", (CBLK, 128, TOK), BF16)
    dl_scr = dscr("dl_scr", (CBLK, 128, TOK), BF16)
    g_scr = dscr("g_scr", (CBLK, 128, TOK), BF16)
    xc_scr = dscr("xc_scr", (DBLK, 128, HTOK), F32)
    flat_scr = dscr("flat_scr", (DBLK, 128, HTOK), BF16)
    facc_scr = dscr("facc_scr", (DBLK, 128, HTOK), F32)
    scr_map = dict(x_str=x_str, u_scr=u_scr, sz_scr=sz_scr, dl_scr=dl_scr,
                   g_scr=g_scr, xc_scr=xc_scr, flat_scr=flat_scr, facc_scr=facc_scr)

    p = Prog(nc)
    S_out = Buf("S_out")
    S_xstr, S_u, S_sz, S_dl, S_g, S_xc, S_flat, S_facc = (
        Buf("S_xstr"), Buf("S_u"), Buf("S_sz"), Buf("S_dl"), Buf("S_g"),
        Buf("S_xc"), Buf("S_flat"), Buf("S_facc"))
    scr_buf = dict(x_str=S_xstr, u_scr=S_u, sz_scr=S_sz, dl_scr=S_dl,
                   g_scr=S_g, xc_scr=S_xc, flat_scr=S_flat, facc_scr=S_facc)

    ones_f, B_ones_f = p.sb(None, "ones_f", [128, 128], F32)
    ones_b, B_ones_b = p.sb(None, "ones_b", [128, 128], BF16)
    sel16, B_sel16 = p.sb(None, "sel16", [NST, NST * 128], BF16)
    ident, B_ident = p.sb(None, "ident", [128, 128], F32)
    rb_sb, B_rb = p.sb(None, "rb_sb", [NST, 1], F32)
    negA, B_negA = p.sb(None, "negA", [128, CBLK * NST], F32)
    convw_sb, B_convw = p.sb(None, "convw_sb", [128, CBLK * 4], F32)
    convb_sb, B_convb = p.sb(None, "convb_sb", [128, CBLK], F32)
    dpar_sb, B_dpar = p.sb(None, "dpar_sb", [128, CBLK], F32)
    dt_sb, B_dt = p.sb(None, "dt_sb", [65, TOK], BF16)
    Brow, B_Brow = p.sb(None, "Brow", [NST, TOK], BF16)
    Crow, B_Crow = p.sb(None, "Crow", [NST, TOK], BF16)
    hcar, B_hcar = p.sb(None, "hcar", [128, CBLK * NST], F32)
    comb_sb, B_comb = p.sb(None, "comb_sb", [NST, HTOK], BF16)
    tb_sb, B_tb = p.sb(None, "tb_sb", [1, 1], mybir.dt.int32)
    tok_cell = {}

    p.dma("sync", ones_f[:], ones_in, writes=[B_ones_f])
    p.dma("gpsimd", ones_b[:], ones_in, writes=[B_ones_b])
    p.dma("gpsimd", sel16[:], sel16_in, writes=[B_sel16])
    p.dma("sync", ident[:], ident_in, writes=[B_ident])
    p.dma("sync", rb_sb[:], rb_in, writes=[B_rb])

    def ln_ws(sc, tag):
        d = {}
        d["ps_s"], d["Bps_s"] = p.ps(sc, f"{tag}_ps", [128, FD])
        d["ps_q"], d["Bps_q"] = p.ps(sc, f"{tag}_pq", [128, FD])
        d["sq"], d["Bsq"] = p.sb(sc, f"{tag}_sq", [128, FD], F32)
        d["vec"], d["Bvec"] = p.sb(sc, f"{tag}_vec", [1, 4 * FD], F32)
        d["bc"], d["Bbc"] = p.ps(sc, f"{tag}_bc", [128, 2 * FD])
        d["outt"], d["Bout"] = p.sb(sc, f"{tag}_out", [128, FD], F32)
        return d

    def feature_ln(ws, tiles, bufs, w, consume):
        ps_s, Bps_s = ws["ps_s"], ws["Bps_s"]
        ps_q, Bps_q = ws["ps_q"], ws["Bps_q"]
        sq, Bsq = ws["sq"], ws["Bsq"]
        vec, Bvec = ws["vec"], ws["Bvec"]
        bc, Bbc = ws["bc"], ws["Bbc"]
        outt, Bout = ws["outt"], ws["Bout"]

        p.mmg([lambda e, i=i, ap=tiles[i]: e.matmul(ps_s[0:1, 0:w], ones_f[:, 0:1], ap[:, 0:w],
                                                    start=(i == 0), stop=(i == DBLK - 1))
               for i in range(DBLK)], reads=[B_ones_f] + bufs, writes=[Bps_s])
        for i in range(DBLK):
            p.op("scalar", lambda e, ap=tiles[i]: e.activation(sq[:, 0:w], ap[:, 0:w], AF.Square),
                 reads=[bufs[i]], writes=[Bsq])
            p.mmg([lambda e, i=i: e.matmul(ps_q[0:1, 0:w], ones_f[:, 0:1], sq[:, 0:w],
                                           start=(i == 0), stop=(i == DBLK - 1))],
                  reads=[B_ones_f, Bsq], writes=[Bps_q])
        inv = 1.0 / D_MODEL
        vm = vec[0:1, 0 * FD:0 * FD + w]
        vv = vec[0:1, 1 * FD:1 * FD + w]
        vr = vec[0:1, 2 * FD:2 * FD + w]
        vmr = vec[0:1, 3 * FD:3 * FD + w]
        p.op("vector", lambda e: e.tensor_scalar_mul(vm, ps_s[0:1, 0:w], inv),
             reads=[Bps_s], writes=[Bvec])
        p.op("vector", lambda e: e.tensor_scalar_mul(vv, ps_q[0:1, 0:w], inv),
             reads=[Bps_q], writes=[Bvec])
        p.op("vector", lambda e: e.tensor_tensor(vr, vm, vm, ALU.mult),
             reads=[Bvec], writes=[Bvec])
        p.op("vector", lambda e: e.tensor_tensor(vv, vv, vr, ALU.subtract),
             reads=[Bvec], writes=[Bvec])
        p.op("vector", lambda e: e.tensor_scalar_add(vv, vv, EPS),
             reads=[Bvec], writes=[Bvec])
        p.op("scalar", lambda e: e.activation(vr, vv, AF.Sqrt),
             reads=[Bvec], writes=[Bvec])
        p.op("vector", lambda e: e.reciprocal(vr, vr),
             reads=[Bvec], writes=[Bvec])
        p.op("vector", lambda e: e.tensor_tensor(vmr, vm, vr, ALU.mult),
             reads=[Bvec], writes=[Bvec])
        p.mmg([lambda e: e.matmul(bc[:, 0:w], ones_f[0:1, :], vr, start=True, stop=True),
               lambda e: e.matmul(bc[:, FD:FD + w], ones_f[0:1, :], vmr, start=True, stop=True)],
              reads=[B_ones_f, Bvec], writes=[Bbc])
        for i in range(DBLK):
            p.op("vector", lambda e, ap=tiles[i]: e.tensor_tensor(outt[:, 0:w], ap[:, 0:w], bc[:, 0:w], ALU.mult),
                 reads=[bufs[i], Bbc], writes=[Bout])
            p.op("vector", lambda e: e.tensor_tensor(outt[:, 0:w], outt[:, 0:w], bc[:, FD:FD + w], ALU.subtract),
                 reads=[Bbc], writes=[Bout])
            consume(i, outt, Bout)

    with p.scope() as sc:
        tiles, bufs = [], []
        for i in range(DBLK):
            t, b = p.sb(sc, f"ln1_x{i}", [128, FD], F32)
            tiles.append(t); bufs.append(b)
        ws1 = ln_ws(sc, "ln1")
        for fd in range(NFD):
            for i in range(DBLK):
                p.dma("sync", tiles[i][:, 0:FD], xT[i * 128:(i + 1) * 128, fd * FD:(fd + 1) * FD],
                      writes=[bufs[i]])

            def consume(i, outt, Bout, fd=fd):
                p.dma("sync", x_str[i, :, fd * FD:(fd + 1) * FD], outt[:, 0:FD],
                      reads=[Bout], writes=[S_xstr])
            feature_ln(ws1, tiles, bufs, FD, consume)
        p.barrier()

    def do_layer(lay):
        p.dma("sync", negA[:], a_log[lay].rearrange("p b n -> p (b n)"), writes=[B_negA])
        p.dma("sync", convw_sb[:], conv_w[lay].rearrange("p b k -> p (b k)"), writes=[B_convw])
        p.dma("sync", convb_sb[:], conv_b[lay], writes=[B_convb])
        p.dma("sync", dpar_sb[:], d_par[lay], writes=[B_dpar])
        p.op("scalar", lambda e: e.activation(negA[:], negA[:], AF.Exp), reads=[B_negA], writes=[B_negA])
        p.op("vector", lambda e: e.tensor_scalar_mul(negA[:], negA[:], -1.0), reads=[B_negA], writes=[B_negA])
        p.op("vector", lambda e: e.memset(hcar[:], 0.0), writes=[B_hcar])
        p.op("vector", lambda e: e.memset(dt_sb[64:65, :], 1.0), writes=[B_dt])

        with p.scope() as sc:
            x_bf, B_xbf = [], []
            for i in range(DBLK):
                t, b = p.sb(sc, f"xbf{i}", [128, TOK], BF16)
                x_bf.append(t); B_xbf.append(b)
                p.dma("gpsimd", t[:], x_str[i], reads=[S_xstr], writes=[b])
            wsl, B_wsl = [], []
            for i in range(2):
                t, b = p.sb(sc, f"wisl{i}", [128, DBLK * 128], BF16)
                wsl.append(t); B_wsl.append(b)
            xin, B_xin = [], []
            for i in range(2):
                t, b = p.sb(sc, f"xin{i}", [128, 4 + TOK], BF16)
                xin.append(t); B_xin.append(b)
            cacc, B_cacc = [], []
            for i in range(2):
                t, b = p.sb(sc, f"cacc{i}", [128, TOK], F32)
                cacc.append(t); B_cacc.append(b)
            u_t, B_u = [], []
            for i in range(2):
                t, b = p.sb(sc, f"u_t{i}", [128, TOK], BF16)
                u_t.append(t); B_u.append(b)
            zt, B_zt = [], []
            for i in range(2):
                t, b = p.sb(sc, f"zt{i}", [128, FD], BF16)
                zt.append(t); B_zt.append(b)
            pbank, B_pbank = [], []
            for i in range(4):
                t, b = p.ps(sc, f"ph1b{i}", [128, FD])
                pbank.append(t); B_pbank.append(b)

            for m in range(EBLK):
                sl = m % 2
                p.dma("gpsimd", wsl[sl][:].rearrange("p (k j) -> p k j", k=DBLK), wi_t[lay, m].rearrange("k p j -> p k j"),
                      writes=[B_wsl[sl]])
                is_x = m < CBLK
                dst, B_dst = (xin[m % 2], B_xin[m % 2]) if is_x else (None, None)
                for fd in range(NFD):
                    bk = 2 * sl + (fd % 2)
                    p.mmg([
                        (lambda e, kb=kb, sl=sl, bk=bk, fd=fd:
                         e.matmul(pbank[bk][:, 0:FD], wsl[sl][:, kb * 128:(kb + 1) * 128],
                                  x_bf[kb][:, fd * FD:(fd + 1) * FD],
                                  start=(kb == 0), stop=(kb == DBLK - 1)))
                        for kb in range(DBLK)],
                        reads=[B_wsl[sl]] + B_xbf, writes=[B_pbank[bk]])
                    if is_x:
                        p.op("scalar", lambda e, bk=bk, d=dst, fd=fd:
                             e.copy(d[:, 4 + fd * FD:4 + (fd + 1) * FD], pbank[bk][:, 0:FD]),
                             reads=[B_pbank[bk]], writes=[B_dst])
                    else:
                        blkz = m - CBLK
                        p.op("scalar", lambda e, bk=bk, z=zt[fd % 2]:
                             e.activation(z[:, 0:FD], pbank[bk][:, 0:FD], AF.Silu),
                             reads=[B_pbank[bk]], writes=[B_zt[fd % 2]])
                        p.dma("sync", sz_scr[blkz, :, fd * FD:(fd + 1) * FD], zt[fd % 2][:, 0:FD],
                              reads=[B_zt[fd % 2]], writes=[S_sz])
                if is_x:
                    blk = m
                    acc, B_acc = cacc[m % 2], B_cacc[m % 2]
                    p.op("gpsimd", lambda e, d=dst: e.memset(d[:, 0:4], 0.0),
                         writes=[B_dst])
                    p.op("vector", lambda e, d=dst, acc=acc, c=blk * 4:
                         e.tensor_scalar_mul(acc[:, 0:TOK], d[:, 1:1 + TOK], convw_sb[:, c:c + 1]),
                         reads=[B_dst, B_convw], writes=[B_acc])
                    for k in (1, 2, 3):
                        p.op("vector", lambda e, d=dst, acc=acc, c=blk * 4 + k, k=k:
                             e.scalar_tensor_tensor(acc[:, 0:TOK], d[:, 1 + k:1 + k + TOK],
                                                    convw_sb[:, c:c + 1], acc[:, 0:TOK],
                                                    ALU.mult, ALU.add),
                             reads=[B_dst, B_convw], writes=[B_acc])
                    ut, B_ut = u_t[m % 2], B_u[m % 2]
                    p.op("scalar", lambda e, acc=acc, ut=ut, blk=blk:
                         e.activation(ut[:, 0:TOK], acc[:, 0:TOK], AF.Silu, bias=convb_sb[:, blk:blk + 1]),
                         reads=[B_acc, B_convb], writes=[B_ut])
                    p.dma("sync", u_scr[blk], ut[:, 0:TOK], reads=[B_ut], writes=[S_u])
            p.barrier()

        if stop_after < 2:
            return
        if stop_after < 4:
            return
        with p.scope() as sc:
            sc3 = sc.enter_context(p.scope())
            wxs, B_wxs = p.sb(sc3, "wxs", [128, CBLK * 96], BF16)
            p.dma("gpsimd", wxs[:].rearrange("p (b m) -> p b m", b=CBLK),
                  wx_t[lay].rearrange("b p m -> p b m"), writes=[B_wxs])
            ur, B_ur = [], []
            for i in range(4):
                t, b = p.sb(sc3, f"ph3u{i}", [128, HALF // 2], BF16)
                ur.append(t); B_ur.append(b)
            psA, B_psA = p.ps(sc3, "ph3A", [128, HALF])
            psC, B_psC = p.ps(sc3, "ph3C", [128, HALF])
            QU = HALF // 2
            for h in range(2):
                t0_ = h * HALF
                for q_ in range(2):
                  for kb in range(CBLK):
                    p.dma("gpsimd", ur[kb % 4][:], u_scr[kb, :, t0_ + q_ * QU:t0_ + (q_ + 1) * QU],
                          reads=[S_u], writes=[B_ur[kb % 4]])
                    for fd in range(QU // FD):
                        fda = q_ * (QU // FD) + fd
                        p.mmg([(lambda e, kb=kb, fd=fd, fda=fda, sl=kb % 4:
                                e.matmul(psA[0:80, fda * FD:(fda + 1) * FD],
                                         wxs[:, kb * 96:kb * 96 + 80],
                                         ur[sl][:, fd * FD:(fd + 1) * FD],
                                         start=(kb == 0), stop=(kb == CBLK - 1)))],
                              reads=[B_wxs, B_ur[kb % 4]], writes=[B_psA])
                        p.mmg([(lambda e, kb=kb, fd=fd, fda=fda, sl=kb % 4:
                                e.matmul(psC[0:16, fda * FD:(fda + 1) * FD],
                                         wxs[:, kb * 96 + 80:kb * 96 + 96],
                                         ur[sl][:, fd * FD:(fd + 1) * FD],
                                         start=(kb == 0), stop=(kb == CBLK - 1)))],
                              reads=[B_wxs, B_ur[kb % 4]], writes=[B_psC])
                p.op("vector", lambda e, t0_=t0_: e.tensor_copy(dt_sb[0:64, t0_:t0_ + HALF], psA[0:64, :]),
                     reads=[B_psA], writes=[B_dt])
                p.op("vector", lambda e, t0_=t0_: e.tensor_copy(Brow[0:16, t0_:t0_ + HALF], psA[64:80, :]),
                     reads=[B_psA], writes=[B_Brow])
                p.op("scalar", lambda e, t0_=t0_: e.copy(Crow[0:16, t0_:t0_ + HALF], psC[0:16, :]),
                     reads=[B_psC], writes=[B_Crow])
            sc3.close()
            Bb, B_Bb = p.sb(sc, "Bb", [128, NST * TH], BF16)
            Cb, B_Cb = p.sb(sc, "Cb", [128, NST * TH], BF16)
            wdts, B_wdts = p.sb(sc, "wdts", [65, CBLK * 128], BF16)
            p.dma("gpsimd", wdts[:].rearrange("r (b j) -> r b j", b=CBLK),
                  wdt_t[lay].rearrange("b r j -> r b j"), writes=[B_wdts])
            spa2, B_spa2 = [], []
            spb2, B_spb2 = [], []
            for i in range(2):
                t, b = p.sb(sc, f"spa{i}", [128, TH], BF16); spa2.append(t); B_spa2.append(b)
                t, b = p.sb(sc, f"spb{i}", [128, TH], BF16); spb2.append(t); B_spb2.append(b)
            psD, B_psD = [], []
            for i in range(2):
                t, b = p.ps(sc, f"scand{i}", [128, TH])
                psD.append(t); B_psD.append(b)

            psB, B_psB = [], []
            for i in range(2):
                t, b = p.ps(sc, f"scanps{i}", [128, TH])
                psB.append(t); B_psB.append(b)
            dl_b, B_dlb = [], []
            u_b, B_ub = [], []
            sz_b, B_szb = [], []
            for i in range(2):
                t, b = p.sb(sc, f"sdl{i}", [128, TH], BF16); dl_b.append(t); B_dlb.append(b)
                t, b = p.sb(sc, f"su{i}", [128, TH], BF16); u_b.append(t); B_ub.append(b)
                t, b = p.sb(sc, f"ssz{i}", [128, TH], BF16); sz_b.append(t); B_szb.append(b)
            du_b, B_dub = p.sb(sc, "sdu", [128, TH], BF16)
            a_q, B_aq = [], []
            for i in range(2):
                t, b = p.sb(sc, f"aq{i}", [128, 4 * TH], BF16); a_q.append(t); B_aq.append(b)
            b_q, B_bq = p.sb(sc, "bq", [128, 4 * TH], BF16)
            h_q, B_hq = p.sb(sc, "hq", [128, 4 * TH], BF16)
            car4, B_car4 = p.sb(sc, "car4", [128, 4], F32)
            qA, B_qA = p.sb(sc, "qA", [128, 4 * TH], BF16)
            qB, B_qB = p.sb(sc, "qB", [128, 4 * TH], BF16)
            yac, B_yac = p.sb(sc, "yac", [128, TH], F32)
            y2, B_y2 = p.sb(sc, "y2", [128, TH], BF16)
            g_t, B_gt = [], []
            for i in range(2):
                t, b = p.sb(sc, f"gt{i}", [128, TH], BF16); g_t.append(t); B_gt.append(b)
            wosl, B_wosl = [], []
            for i in range(2):
                t, b = p.sb(sc, f"wosl{i}", [128, CBLK * 128], BF16); wosl.append(t); B_wosl.append(b)
            gr, B_gr = [], []
            for kb in range(8):
                t, b = p.sb(sc, f"gr{kb}", [128, FD], BF16); gr.append(t); B_gr.append(b)
            xres, B_xres = [], []
            for i in range(2):
                t, b = p.sb(sc, f"xres{i}", [128, FD], F32); xres.append(t); B_xres.append(b)

            for t_i in range(NTH):
                t0 = t_i * TH
                for n in range(NST):
                    sl = n % 2
                    p.mmg([(lambda e, n=n, sub=sub, sl=sl:
                            e.matmul(psB[sl][:, sub * FD:(sub + 1) * FD],
                                     sel16[:, n * 128:(n + 1) * 128],
                                     Brow[:, t0 + sub * FD:t0 + (sub + 1) * FD],
                                     start=True, stop=True))
                           for sub in range(TH // FD)],
                          reads=[B_sel16, B_Brow, B_Crow], writes=[B_psB[sl]])
                    p.op("vector", lambda e, n=n, sl=sl:
                         e.tensor_copy(Bb[:, n * TH:(n + 1) * TH], psB[sl][:]),
                         reads=[B_psB[sl]], writes=[B_Bb])
                    p.mmg([(lambda e, n=n, sub=sub, sl=sl:
                            e.matmul(psB[sl][:, sub * FD:(sub + 1) * FD],
                                     sel16[:, n * 128:(n + 1) * 128],
                                     Crow[:, t0 + sub * FD:t0 + (sub + 1) * FD],
                                     start=True, stop=True))
                           for sub in range(TH // FD)],
                          reads=[B_sel16, B_Crow], writes=[B_psB[sl]])
                    p.op("scalar", lambda e, n=n, sl=sl:
                         e.copy(Cb[:, n * TH:(n + 1) * TH], psB[sl][:]),
                         reads=[B_psB[sl]], writes=[B_Cb])

                def delta_chain(blk, t0=t0):
                    sl = blk % 2
                    p.mmg([(lambda e, blk=blk, sub=sub, sl=sl:
                            e.matmul(psD[sl][:, sub * FD:(sub + 1) * FD],
                                     wdts[:, blk * 128:(blk + 1) * 128],
                                     dt_sb[:, t0 + sub * FD:t0 + (sub + 1) * FD],
                                     start=True, stop=True))
                           for sub in range(TH // FD)],
                          reads=[B_wdts, B_dt, B_Brow, B_Crow], writes=[B_psD[sl]])
                    p.op("scalar", lambda e, sl=sl: e.activation(spb2[sl][:], psD[sl][:], AF.Relu),
                         reads=[B_psD[sl]], writes=[B_spb2[sl]])
                    p.op("scalar", lambda e, sl=sl: e.activation(spa2[sl][:], psD[sl][:], AF.Abs),
                         reads=[B_psD[sl]], writes=[B_spa2[sl]])
                    p.op("scalar", lambda e, sl=sl: e.activation(spa2[sl][:], spa2[sl][:], AF.Exp, scale=-1.0),
                         reads=[B_spa2[sl]], writes=[B_spa2[sl]])
                    p.op("scalar", lambda e, sl=sl: e.activation(spa2[sl][:], spa2[sl][:], AF.Ln, bias=1.0),
                         reads=[B_spa2[sl]], writes=[B_spa2[sl]])
                    p.op("gpsimd", lambda e, sl=sl: e.tensor_tensor(dl_b[sl][:], spa2[sl][:], spb2[sl][:], ALU.add),
                         reads=[B_spa2[sl], B_spb2[sl]], writes=[B_dlb[sl]])
                    p.dma("sync", u_b[sl][:], u_scr[blk, :, t0:t0 + TH], reads=[S_u], writes=[B_ub[sl]])
                    p.dma("sync", sz_b[sl][:], sz_scr[blk, :, t0:t0 + TH], reads=[S_sz], writes=[B_szb[sl]])

                for pre in range(2):
                    delta_chain(pre)
                for blk in range(CBLK):
                    sl = blk % 2
                    if blk + 2 < CBLK:
                        delta_chain(blk + 2)
                    p.op("vector", lambda e, sl=sl: e.tensor_tensor(du_b[:], dl_b[sl][:], u_b[sl][:], ALU.mult),
                         reads=[B_dlb[sl], B_ub[sl]], writes=[B_dub])
                    for g in range(4):
                        asl = g % 2
                        at, B_at = a_q[asl], B_aq[asl]
                        for j in range(4):
                            n = g * 4 + j
                            p.op("scalar", lambda e, sl=sl, j=j, n=n, at=at, blk=blk:
                                 e.activation(at[:, j * TH:(j + 1) * TH], dl_b[sl][:], AF.Exp,
                                              scale=negA[:, blk * NST + n:blk * NST + n + 1]),
                                 reads=[B_dlb[sl], B_negA], writes=[B_at])
                        dub = du_b[:].rearrange("p (o t) -> p o t", o=1).broadcast_to([128, 4, TH])
                        p.op("vector", lambda e, g=g, dub=dub:
                             e.tensor_tensor(b_q[:].rearrange("p (o t) -> p o t", o=4), dub,
                                             Bb[:, g * 4 * TH:(g + 1) * 4 * TH].rearrange("p (o t) -> p o t", o=4),
                                             ALU.mult),
                             reads=[B_dub, B_Bb], writes=[B_bq])
                        a3 = at[:].rearrange("p (o t) -> p o t", o=4)
                        b3 = b_q[:].rearrange("p (o t) -> p o t", o=4)
                        p.op("vector", lambda e, a3=a3, blk=blk, g=g:
                             e.tensor_tensor(car4[:], a3[:, :, 0:1].squeeze(2),
                                             hcar[:, blk * NST + g * 4:blk * NST + g * 4 + 4], ALU.mult),
                             reads=[B_at, B_hcar], writes=[B_car4])
                        p.op("vector", lambda e, b3=b3:
                             e.tensor_tensor(b3[:, :, 0:1].squeeze(2), b3[:, :, 0:1].squeeze(2),
                                             car4[:], ALU.add),
                             reads=[B_car4], writes=[B_bq])
                        p.op("vector", lambda e, a3=a3: e.memset(a3[:, :, 0:1], 0.0),
                             writes=[B_at])
                        p.op("vector", lambda e, at=at:
                             e.tensor_tensor_scan(h_q[:], at[:], b_q[:], 0.0, ALU.mult, ALU.add),
                             reads=[B_at, B_bq], writes=[B_hq])
                        h3 = h_q[:].rearrange("p (o t) -> p o t", o=4)
                        p.op("vector", lambda e, h3=h3, blk=blk, g=g:
                             e.tensor_copy(hcar[:, blk * NST + g * 4:blk * NST + g * 4 + 4],
                                           h3[:, :, TH - 1:TH].squeeze(2)),
                             reads=[B_hq], writes=[B_hcar])
                        qdst, B_qdst = (qA, B_qA) if g == 0 else (qB, B_qB)
                        p.op("vector", lambda e, g=g, qdst=qdst:
                             e.tensor_tensor(qdst[:], h_q[:],
                                             Cb[:, g * 4 * TH:(g + 1) * 4 * TH], ALU.mult),
                             reads=[B_hq, B_Cb], writes=[B_qdst])
                        if g in (1, 2):
                            p.op("vector", lambda e: e.tensor_tensor(qA[:], qA[:], qB[:], ALU.add),
                                 reads=[B_qB], writes=[B_qA])
                    p.op("vector", lambda e: e.tensor_tensor(qB[:], qA[:], qB[:], ALU.add),
                         reads=[B_qA], writes=[B_qB])
                    p.op("vector", lambda e: e.tensor_tensor(qB[:, 0:2 * TH], qB[:, 0:2 * TH],
                                                             qB[:, 2 * TH:4 * TH], ALU.add),
                         reads=[B_qB], writes=[B_qB])
                    p.op("vector", lambda e: e.tensor_tensor(yac[:], qB[:, 0:TH], qB[:, TH:2 * TH], ALU.add),
                         reads=[B_qB], writes=[B_yac])
                    p.op("vector", lambda e, sl=sl, blk=blk:
                         e.scalar_tensor_tensor(y2[:], u_b[sl][:], dpar_sb[:, blk:blk + 1], yac[:],
                                                ALU.mult, ALU.add),
                         reads=[B_ub[sl], B_dpar, B_yac], writes=[B_y2])
                    gsl = blk % 2
                    p.op("gpsimd", lambda e, sl=sl, gsl=gsl:
                         e.tensor_tensor(g_t[gsl][:], y2[:], sz_b[sl][:], ALU.mult),
                         reads=[B_y2, B_szb[sl]], writes=[B_gt[gsl]])
                    p.dma("sync", g_scr[blk, :, t0:t0 + TH], g_t[gsl][:],
                          reads=[B_gt[gsl]], writes=[S_g])
                for fdl in range(TH // FD):
                    pslots = [psB[0][:, 0:FD], psB[0][:, FD:2 * FD],
                              psB[1][:, 0:FD], psB[1][:, FD:2 * FD],
                              psD[0][:, 0:FD], psD[0][:, FD:2 * FD],
                              psD[1][:, 0:FD], psD[1][:, FD:2 * FD]]
                    pbufs = [B_psB[0], B_psB[0], B_psB[1], B_psB[1],
                             B_psD[0], B_psD[0], B_psD[1], B_psD[1]]
                    for kh in range(2):
                        for kb in range(8):
                            p.dma("sync", gr[kb][:],
                                  g_scr[kh * 8 + kb, :, t0 + fdl * FD:t0 + (fdl + 1) * FD],
                                  reads=[S_g], writes=[B_gr[kb]])
                        for m in range(DBLK):
                            wsl_ = m % 2
                            p.dma("gpsimd",
                                  wosl[wsl_][:, 0:8 * 128].rearrange("p (k j) -> p k j", k=8),
                                  wo_t[lay, m, kh * 8:(kh + 1) * 8].rearrange("k p j -> p k j"),
                                  writes=[B_wosl[wsl_]])
                            p.mmg([(lambda e, kb=kb, m=m, wsl_=wsl_, kh=kh:
                                    e.matmul(pslots[m], wosl[wsl_][:, kb * 128:(kb + 1) * 128],
                                             gr[kb][:, 0:FD], start=(kh == 0 and kb == 0),
                                             stop=(kh == 1 and kb == 7)))
                                   for kb in range(8)],
                                  reads=[B_wosl[wsl_]] + B_gr, writes=[pbufs[m]])
                    for m in range(DBLK):
                        xsl = m % 2
                        p.op("scalar", lambda e, xsl=xsl, m=m:
                             e.copy(xres[xsl][:], pslots[m]),
                             reads=[pbufs[m]], writes=[B_xres[xsl]])
                        p.dma("gpsimd", x_str[m, :, t0 + fdl * FD:t0 + (fdl + 1) * FD], xres[xsl][:],
                              reads=[B_xres[xsl]], writes=[S_xstr], accum_op=ALU.add)
            p.barrier()

        if stop_after < 5:
            return

    if stop_after >= 1:
        for lay in range(NL):
            do_layer(lay)

    def do_combine1():
        p.dma("sync", tb_sb[:], tok_base, writes=[B_tb])
        p._deps("sync", [B_tb], [])
        def _load_tok(e):
            r = e.register("tokreg").__enter__()
            e.reg_load(r, tb_sb[0:1, 0:1])
            sv = e.snap(r, min_val=0, max_val=HTOK)
            for fd in range(NFD_F):
                tok_cell[fd] = e.snap(sv + fd * FD, min_val=fd * FD, max_val=HTOK + fd * FD)
        p.q["sync"].append(_load_tok)
        with p.scope() as sc:
            mt_all, B_mta = p.sb(sc, "c1m", [128, DBLK * FD], F32)
            x0_all, B_x0a = p.sb(sc, "c1xorig", [128, DBLK * FD], F32)
            mt = [mt_all[:, i * FD:(i + 1) * FD] for i in range(DBLK)]
            B_mt = [B_mta] * DBLK
            xct, B_xct = [], []
            for i in range(DBLK):
                t, bb = p.sb(sc, f"c1x{i}", [128, FD], F32); xct.append(t); B_xct.append(bb)
            msk, B_msk = p.sb(sc, "c1k", [128, FD], F32)
            fbt, B_fbt = p.sb(sc, "c1fb", [128, FD], BF16)
            wsa = ln_ws(sc, "c1a")
            wsb = ln_ws(sc, "c1b")
            for fd in range(NFD_F):
                p.dma("sync", mt_all[:].rearrange("p (b t) -> p b t", b=DBLK),
                      (lambda fd=fd: x_str.rearrange("b p t -> p b t")[:, :, bass.ds(tok_cell[fd], FD)]),
                      reads=[S_xstr], writes=[B_mta])
                p.dma("sync", x0_all[:].rearrange("p (b t) -> p b t", b=DBLK),
                      (lambda fd=fd: xT.rearrange("(b p) t -> p b t", p=128)[:, :, bass.ds(tok_cell[fd], FD)]),
                      writes=[B_x0a])
                for i in range(DBLK):
                    p.op("vector", lambda e, i=i: e.tensor_tensor(msk[:], mt[i], mt[i], ALU.is_equal),
                         reads=[B_mta], writes=[B_msk])
                    p.op("vector", lambda e, i=i: e.tensor_scalar_max(mt[i], mt[i], -FMAX),
                         reads=[B_msk], writes=[B_mta])
                    p.op("vector", lambda e, i=i: e.tensor_scalar_min(mt[i], mt[i], FMAX),
                         reads=[], writes=[B_mta])
                    p.op("vector", lambda e, i=i: e.tensor_tensor(mt[i], mt[i], msk[:], ALU.mult),
                         reads=[B_msk], writes=[B_mta])

                def consume1(i, outt, Bout, fd=fd):
                    p.op("vector", lambda e, i=i: e.tensor_tensor(xct[i][:], x0_all[:, i * FD:(i + 1) * FD],
                                                                  outt[:, 0:FD], ALU.add),
                         reads=[B_x0a, Bout], writes=[B_xct[i]])
                    p.dma("sync", xc_scr[i, :, fd * FD:(fd + 1) * FD], xct[i][:],
                          reads=[B_xct[i]], writes=[S_xc])
                feature_ln(wsa, mt, B_mt, FD, consume1)

                def consume2(i, outt, Bout, fd=fd):
                    p.op("vector", lambda e: e.tensor_copy(fbt[:], outt[:, 0:FD]),
                         reads=[Bout], writes=[B_fbt])
                    p.dma("sync", flat_scr[i, :, fd * FD:(fd + 1) * FD], fbt[:],
                          reads=[B_fbt], writes=[S_flat])
                feature_ln(wsb, xct, B_xct, FD, consume2)
            p.barrier()

    if stop_after >= 6:
        do_combine1()

    if ffn and stop_after >= 7:
        def do_router():
          with p.scope() as sc:
            rws, B_rws = p.sb(sc, "rws", [128, DBLK * NST], BF16)
            p.dma("gpsimd", rws[:].rearrange("p (b m) -> p b m", b=DBLK), rw_t.rearrange("b p m -> p b m"), writes=[B_rws])
            fb, B_fb = [], []
            for i in range(DBLK):
                t, b = p.sb(sc, f"rfb{i}", [128, FD], BF16)
                fb.append(t); B_fb.append(b)
            psl, B_psl = p.ps(sc, "rps", [128, FD])
            st_, B_st = p.sb(sc, "rst", [NST, FD], F32)
            pst, B_pst = p.ps(sc, "rpt", [128, NST])
            stt, B_stt = p.sb(sc, "rstt", [128, NST], F32)
            m1, B_m1 = p.sb(sc, "rm1", [128, 1], F32)
            m2, B_m2 = p.sb(sc, "rm2", [128, 1], F32)
            mk1, B_mk1 = p.sb(sc, "rmk1", [128, NST], F32)
            mk2, B_mk2 = p.sb(sc, "rmk2", [128, NST], F32)
            w1, B_w1 = p.sb(sc, "rw1", [128, 1], F32)
            w2, B_w2 = p.sb(sc, "rw2", [128, 1], F32)
            ct, B_ct = p.sb(sc, "rct", [128, NST], F32)
            cps, B_cps = p.ps(sc, "rcps", [128, 128])
            for fd in range(NFD_F):
                for i in range(DBLK):
                    p.dma("sync", fb[i][:], flat_scr[i, :, fd * FD:(fd + 1) * FD],
                          reads=[S_flat], writes=[B_fb[i]])
                p.mmg([(lambda e, kb=kb:
                        e.matmul(psl[0:NST, 0:FD], rws[:, kb * NST:(kb + 1) * NST], fb[kb][:, 0:FD],
                                 start=(kb == 0), stop=(kb == DBLK - 1)))
                       for kb in range(DBLK)],
                      reads=[B_rws] + B_fb, writes=[B_psl])
                p.op("vector", lambda e: e.tensor_scalar_add(st_[:, 0:FD], psl[0:NST, 0:FD], rb_sb[0:NST, 0:1]),
                     reads=[B_psl, B_rb], writes=[B_st])
                for ch in range(FD // 128):
                    p.mmg([lambda e, ch=ch: e.transpose(pst[:, 0:NST], st_[0:NST, ch * 128:(ch + 1) * 128], ident[0:NST, 0:NST])],
                          reads=[B_st, B_ident], writes=[B_pst])
                    p.op("vector", lambda e: e.tensor_copy(stt[:], pst[:, 0:NST]),
                         reads=[B_pst], writes=[B_stt])
                    p.op("vector", lambda e: e.tensor_reduce(m1[:], stt[:], AX.X, ALU.max),
                         reads=[B_stt], writes=[B_m1])
                    p.op("vector", lambda e: e.tensor_single_scalar(mk1[:], stt[:], m1[:, 0:1], ALU.is_equal),
                         reads=[B_stt, B_m1], writes=[B_mk1])
                    p.op("vector", lambda e: e.scalar_tensor_tensor(mk2[:], mk1[:], -1e30, stt[:], ALU.mult, ALU.add),
                         reads=[B_mk1, B_stt], writes=[B_mk2])
                    p.op("vector", lambda e: e.tensor_reduce(m2[:], mk2[:], AX.X, ALU.max),
                         reads=[B_mk2], writes=[B_m2])
                    p.op("vector", lambda e: e.tensor_single_scalar(mk2[:], mk2[:], m2[:, 0:1], ALU.is_equal),
                         reads=[B_m2], writes=[B_mk2])
                    p.op("vector", lambda e: e.tensor_tensor(w1[:], m1[:], m2[:], ALU.subtract),
                         reads=[B_m1, B_m2], writes=[B_w1])
                    p.op("scalar", lambda e: e.activation(w1[:], w1[:], AF.Sigmoid),
                         reads=[B_w1], writes=[B_w1])
                    p.op("vector", lambda e: e.tensor_scalar(w2[:], w1[:], -1.0, 1.0, ALU.mult, ALU.add),
                         reads=[B_w1], writes=[B_w2])
                    p.op("vector", lambda e: e.tensor_scalar_mul(ct[:], mk1[:], w1[:, 0:1]),
                         reads=[B_mk1, B_w1], writes=[B_ct])
                    p.op("vector", lambda e: e.scalar_tensor_tensor(ct[:], mk2[:], w2[:, 0:1], ct[:], ALU.mult, ALU.add),
                         reads=[B_mk2, B_w2], writes=[B_ct])
                    p.mmg([lambda e: e.transpose(cps[0:NST, 0:128], ct[:], ident[:])],
                          reads=[B_ct, B_ident], writes=[B_cps])
                    p.op("vector", lambda e, fd=fd, ch=ch:
                         e.tensor_copy(comb_sb[0:NST, fd * FD + ch * 128:fd * FD + (ch + 1) * 128], cps[0:NST, 0:128]),
                         reads=[B_cps], writes=[B_comb])
            p.barrier()

        if stop_after >= 7:
            do_router()

        def do_experts():
          with p.scope() as sc:
            fb, B_fb = [], []
            for i in range(DBLK):
                t, b = p.sb(sc, f"efb{i}", [128, HTOK], BF16)
                fb.append(t); B_fb.append(b)
                p.dma("gpsimd", t[:], flat_scr[i], reads=[S_flat], writes=[b])
            upsl, B_upsl = p.sb(sc, "upsl", [128, 4 * DBLK * 128], BF16)
            gpsl, B_gpsl = p.sb(sc, "gpsl", [128, 4 * DBLK * 128], BF16)
            dnsl, B_dnsl = p.sb(sc, "dnsl", [128, DBLK * 4 * 128], BF16)

            hid, B_hid = [], []
            for m in range(4):
                t, b = p.sb(sc, f"hid{m}", [128, HTOK], BF16)
                hid.append(t); B_hid.append(b)
            psu, B_psu = [], []
            for i in range(4):
                t, b = p.ps(sc, f"eps{i}", [128, FD])
                psu.append(t); B_psu.append(b)
            sgt, B_sgt = p.sb(sc, "sgt", [128, FD], BF16)
            cbp, B_cbp = p.ps(sc, "cbp", [128, FD])
            ot, B_ot = [], []
            for i in range(2):
                t, b = p.sb(sc, f"eot{i}", [128, FD], F32)
                ot.append(t); B_ot.append(b)

            for k in range(NEXP + 2):
                shared = (k >= NEXP)
                s = k - NEXP
                if shared:
                    p.dma("gpsimd", gpsl[:].rearrange("p (m k j) -> p m k j", m=4, k=DBLK),
                          shg_t[4 * s:4 * s + 4].rearrange("m k p j -> p m k j"), writes=[B_gpsl])
                    p.dma("gpsimd", upsl[:].rearrange("p (m k j) -> p m k j", m=4, k=DBLK),
                          shu_t[4 * s:4 * s + 4].rearrange("m k p j -> p m k j"), writes=[B_upsl])
                    p.dma("gpsimd", dnsl[:].rearrange("p (m k j) -> p m k j", m=DBLK, k=4),
                          shd_t[s].rearrange("m k p j -> p m k j"), writes=[B_dnsl])
                else:
                    p.dma("gpsimd", gpsl[:].rearrange("p (m k j) -> p m k j", m=4, k=DBLK),
                          gp_t[k].rearrange("m k p j -> p m k j"), writes=[B_gpsl])
                    p.dma("gpsimd", upsl[:].rearrange("p (m k j) -> p m k j", m=4, k=DBLK),
                          up_t[k].rearrange("m k p j -> p m k j"), writes=[B_upsl])
                    p.dma("gpsimd", dnsl[:].rearrange("p (m k j) -> p m k j", m=DBLK, k=4),
                          dn_t[k].rearrange("m k p j -> p m k j"), writes=[B_dnsl])

                for m in range(4):
                    for fd in range(NFD_F):
                        pg, Bpg = (psu[0], B_psu[0]) if fd % 2 == 0 else (psu[1], B_psu[1])
                        pu, Bpu = (psu[2], B_psu[2]) if fd % 2 == 0 else (psu[3], B_psu[3])
                        p.mmg([(lambda e, kb=kb, m=m, fd=fd, pg=pg:
                                e.matmul(pg[:, 0:FD], gpsl[:, (m * DBLK + kb) * 128:(m * DBLK + kb + 1) * 128],
                                         fb[kb][:, fd * FD:(fd + 1) * FD],
                                         start=(kb == 0), stop=(kb == DBLK - 1)))
                               for kb in range(DBLK)],
                              reads=[B_gpsl] + B_fb, writes=[Bpg])
                        p.mmg([(lambda e, kb=kb, m=m, fd=fd, pu=pu:
                                e.matmul(pu[:, 0:FD], upsl[:, (m * DBLK + kb) * 128:(m * DBLK + kb + 1) * 128],
                                         fb[kb][:, fd * FD:(fd + 1) * FD],
                                         start=(kb == 0), stop=(kb == DBLK - 1)))
                               for kb in range(DBLK)],
                              reads=[B_upsl] + B_fb, writes=[Bpu])
                        p.op("scalar", lambda e, pg=pg: e.activation(sgt[:], pg[:, 0:FD], AF.Silu),
                             reads=[Bpg], writes=[B_sgt])
                        p.op("vector", lambda e, m=m, fd=fd, pu=pu:
                             e.tensor_tensor(hid[m][:, fd * FD:(fd + 1) * FD], sgt[:], pu[:, 0:FD], ALU.mult),
                             reads=[B_sgt, Bpu], writes=[B_hid[m]])
                if not shared:
                    for fd in range(NFD_F):
                        p.mmg([lambda e, fd=fd, k=k:
                               e.matmul(cbp[:, 0:FD], sel16[:, k * 128:(k + 1) * 128],
                                        comb_sb[:, fd * FD:(fd + 1) * FD],
                                        start=True, stop=True)],
                              reads=[B_sel16, B_comb], writes=[B_cbp])
                        for m in range(4):
                            p.op("vector", lambda e, m=m, fd=fd:
                                 e.tensor_tensor(hid[m][:, fd * FD:(fd + 1) * FD],
                                                 hid[m][:, fd * FD:(fd + 1) * FD], cbp[:, 0:FD], ALU.mult),
                                 reads=[B_cbp], writes=[B_hid[m]])
                for m2 in range(DBLK):
                    for fd in range(NFD_F):
                        po, Bpo = (psu[0], B_psu[0]) if fd % 2 == 0 else (psu[1], B_psu[1])
                        p.mmg([(lambda e, kb=kb, m2=m2, fd=fd, po=po:
                                e.matmul(po[:, 0:FD], dnsl[:, (m2 * 4 + kb) * 128:(m2 * 4 + kb + 1) * 128],
                                         hid[kb][:, fd * FD:(fd + 1) * FD],
                                         start=(kb == 0), stop=(kb == 3)))
                               for kb in range(4)],
                              reads=[B_dnsl] + B_hid, writes=[Bpo])
                        osl = fd % 2
                        p.op("vector", lambda e, po=po, osl=osl: e.tensor_copy(ot[osl][:], po[:, 0:FD]),
                             reads=[Bpo], writes=[B_ot[osl]])
                        if k == 0:
                            p.dma("gpsimd", facc_scr[m2, :, fd * FD:(fd + 1) * FD], ot[osl][:],
                                  reads=[B_ot[osl]], writes=[S_facc])
                        else:
                            p.dma("gpsimd", facc_scr[m2, :, fd * FD:(fd + 1) * FD], ot[osl][:],
                                  reads=[B_ot[osl]], writes=[S_facc], accum_op=ALU.add)
            p.barrier()

        do_experts()

    def do_combine2():
      with p.scope() as sc:
        ft, B_ft = [], []
        for i in range(DBLK):
            t, b = p.sb(sc, f"c2f{i}", [128, FD], F32)
            ft.append(t); B_ft.append(b)
        xct, B_xct2 = p.sb(sc, "c2x", [128, FD], F32)
        ws2 = ln_ws(sc, "c2")
        for fd in range(NFD_F):
            for i in range(DBLK):
                src = facc_scr if ffn else xc_scr
                sbf = S_facc if ffn else S_xc
                p.dma("sync", ft[i][:], src[i, :, fd * FD:(fd + 1) * FD],
                      reads=[sbf], writes=[B_ft[i]])

            def consume3(i, outt, Bout, fd=fd):
                p.dma("sync", xct[:], xc_scr[i, :, fd * FD:(fd + 1) * FD],
                      reads=[S_xc], writes=[B_xct2])
                p.op("vector", lambda e: e.tensor_tensor(xct[:], xct[:], outt[:, 0:FD], ALU.add),
                     reads=[Bout], writes=[B_xct2])
                p.dma("sync", outT[i * 128:(i + 1) * 128, fd * FD:(fd + 1) * FD], xct[:],
                      reads=[B_xct2], writes=[S_out])
            feature_ln(ws2, ft, B_ft, FD, consume3)

    if stop_after >= 9:
        do_combine2()

    for nmm in dumps:
        src = scr_map[nmm]
        shp = list(src.shape)
        dt = src.dtype
        dst = nc.dram_tensor(f"dump_{nmm}", shp, dt, kind="ExternalOutput").ap()
        p.barrier()
        p.dma("sync", dst, src, reads=[scr_buf[nmm]], writes=[S_out])

    p.barrier()

    p.finalize()
    return nc


_CACHE = {}


def kernel(**inputs):
    host = prepare_host_inputs(inputs)
    nc = build()
    htok = 2048
    in_maps = []
    for core in range(8):
        b, h = core // 2, core % 2
        m = {k: v for k, v in host.items() if k != "xT"}
        m["xT"] = host["xT"][b]
        m["tok_base"] = np.full((1, 1), h * htok, np.int32)
        in_maps.append(m)
    res = run_bass_kernel_spmd(nc, in_maps, core_ids=list(range(8)))
    full = np.empty((4, 4096, 1024), np.float32)
    for core in range(8):
        b, h = core // 2, core % 2
        full[b, h * htok:(h + 1) * htok, :] = np.asarray(res.results[core]["outT"]).T
    return full


# version 15
# speedup vs baseline: 1.0258x; 1.0258x over previous
import numpy as np
from contextlib import ExitStack

import concourse.bass as bass
import concourse.mybir as mybir
from concourse.bass_utils import run_bass_kernel_spmd

F32 = mybir.dt.float32
BF16 = mybir.dt.bfloat16
AF = mybir.ActivationFunctionType
ALU = mybir.AluOpType
AX = mybir.AxisListType

D_MODEL = 1024
D_INNER = 2048
E2 = 4096
NST = 16
NLAYER = 8
NEXP = 16
EPS = 1e-5

DBLK = D_MODEL // 128
CBLK = D_INNER // 128
EBLK = E2 // 128
FD = 512
FMAX = 3.4028234663852886e38


def _tile_lhsT(wT):
    K, M = wT.shape
    return np.ascontiguousarray(
        wT.reshape(K // 128, 128, M // 128, 128).transpose(2, 0, 1, 3))


def prepare_host_inputs(ins):
    f = np.float32
    o = {}
    x = np.asarray(ins["L0_attn_x"], f)
    o["xT"] = np.ascontiguousarray(x.transpose(0, 2, 1))

    wi = np.asarray(ins["L0_attn_W_in"], f)
    o["wi_t"] = np.stack([_tile_lhsT(np.ascontiguousarray(wi[l].T)) for l in range(NLAYER)])
    wo = np.asarray(ins["L0_attn_W_out"], f)
    o["wo_t"] = np.stack([_tile_lhsT(np.ascontiguousarray(wo[l].T)) for l in range(NLAYER)])
    wx = np.asarray(ins["L0_attn_W_x"], f)
    o["wx_t"] = np.ascontiguousarray(
        wx.transpose(0, 2, 1).reshape(NLAYER, CBLK, 128, 96))
    wdt = np.asarray(ins["L0_attn_W_dt"], f)
    bdt = np.asarray(ins["L0_attn_b_dt"], f)
    rows = np.concatenate([wdt.transpose(0, 2, 1), bdt[:, None, :]], axis=1)
    o["wdt_t"] = np.ascontiguousarray(
        rows.reshape(NLAYER, 65, CBLK, 128).transpose(0, 2, 1, 3))
    o["conv_w"] = np.ascontiguousarray(
        np.asarray(ins["L0_attn_conv_w"], f).reshape(NLAYER, CBLK, 128, 4).transpose(0, 2, 1, 3))
    o["conv_b"] = np.ascontiguousarray(
        np.asarray(ins["L0_attn_conv_b"], f).reshape(NLAYER, CBLK, 128).transpose(0, 2, 1))
    o["a_log"] = np.ascontiguousarray(
        np.asarray(ins["L0_attn_A_log"], f).reshape(NLAYER, CBLK, 128, NST).transpose(0, 2, 1, 3))
    o["d_par"] = np.ascontiguousarray(
        np.asarray(ins["L0_attn_D_par"], f).reshape(NLAYER, CBLK, 128).transpose(0, 2, 1))

    up = np.asarray(ins["L0_ffn_up_proj"], f)
    o["up_t"] = np.stack([_tile_lhsT(up[k]) for k in range(NEXP)])
    gp = np.asarray(ins["L0_ffn_gate_proj"], f)
    o["gp_t"] = np.stack([_tile_lhsT(gp[k]) for k in range(NEXP)])
    dn = np.asarray(ins["L0_ffn_down_proj"], f)
    o["dn_t"] = np.stack([_tile_lhsT(dn[k]) for k in range(NEXP)])
    o["shg_t"] = _tile_lhsT(np.ascontiguousarray(np.asarray(ins["L0_ffn_sh_gate"], f).T))
    o["shu_t"] = _tile_lhsT(np.ascontiguousarray(np.asarray(ins["L0_ffn_sh_up"], f).T))
    shd_full = _tile_lhsT(np.ascontiguousarray(np.asarray(ins["L0_ffn_sh_down"], f).T))
    o["shd_t"] = np.ascontiguousarray(np.stack([shd_full[:, 0:4], shd_full[:, 4:8]]))
    rw = np.asarray(ins["L0_ffn_router_w"], f)
    o["rw_t"] = np.ascontiguousarray(rw.T.reshape(DBLK, 128, NEXP))
    o["rb"] = np.ascontiguousarray(np.asarray(ins["L0_ffn_router_bias"], f).reshape(NEXP, 1))

    o["ones128"] = np.ones((128, 128), f)
    o["sel16"] = np.ascontiguousarray(np.kron(np.eye(16, dtype=f), np.ones((1, 128), f)))
    o["ident128"] = np.eye(128, dtype=f)
    return o


class Buf:
    __slots__ = ("w", "readers", "name")

    def __init__(self, name):
        self.name = name
        self.w = None
        self.readers = []


class Prog:
    ENGINES = ("sync", "gpsimd", "tensor", "vector", "scalar")
    DEF_SEM = dict(tensor="pe", vector="dve", scalar="act", gpsimd="pool")
    NLANE = 4

    def __init__(self, nc):
        self.nc = nc
        self.q = {e: [] for e in self.ENGINES}
        self.stack = ExitStack()
        self.sems = {}
        self.cnt = {}
        self.hiwater = {e: {} for e in self.ENGINES}
        names = ["pe", "act", "dve", "pool", "st"]
        self.dma_lanes = {}
        for qn, tag in (("sync", "s"), ("gpsimd", "g"), ("scalar", "c")):
            lanes = [f"dma_{tag}{i}" for i in range(self.NLANE)]
            self.dma_lanes[qn] = lanes
            names += lanes
        for s in names:
            self.sems[s] = self.stack.enter_context(nc.semaphore(s))
            self.cnt[s] = 0
        self.dma_rot = {qn: 0 for qn in self.dma_lanes}

    _serial = 0

    def _uniq(self, name):
        Prog._serial += 1
        return f"{name}_{Prog._serial}"

    def scope(self):
        return ExitStack()

    def sb(self, scope, name, shape, dt):
        name = self._uniq(name)
        t = (scope or self.stack).enter_context(self.nc.sbuf_tensor(name, shape, dt))
        return t, Buf(name)

    def ps(self, scope, name, shape):
        name = self._uniq(name)
        t = (scope or self.stack).enter_context(self.nc.psum_tensor(name, shape, F32))
        return t, Buf(name)

    def _wait(self, eng, sem, val):
        if val <= 0:
            return
        hw = self.hiwater[eng]
        if hw.get(sem, 0) >= val:
            return
        hw[sem] = val
        h = self.sems[sem]
        self.q[eng].append(lambda e, h=h, v=val: e.wait_ge(h, v))

    def _deps(self, eng, reads, writes):
        for b in reads:
            if b.w:
                self._wait(eng, b.w[0], b.w[1])
        for b in writes:
            if b.w:
                self._wait(eng, b.w[0], b.w[1])
            for (s, v, src) in b.readers:
                self._wait(eng, s, v)

    def _commit(self, eng, sem, val, reads, writes):
        for b in reads:
            b.readers.append((sem, val, eng))
        for b in writes:
            b.w = (sem, val, eng)
            b.readers = []

    def op(self, eng, fn, reads=(), writes=()):
        self._deps(eng, reads, writes)
        sem = self.DEF_SEM[eng]
        self.cnt[sem] += 1
        v = self.cnt[sem]
        h = self.sems[sem]
        self.q[eng].append(lambda e, fn=fn, h=h: fn(e).then_inc(h, 1))
        self._commit(eng, sem, v, reads, writes)
        return v

    def mmg(self, fns, reads=(), writes=()):
        self._deps("tensor", reads, writes)
        self.cnt["pe"] += 1
        v = self.cnt["pe"]
        h = self.sems["pe"]
        for i, fn in enumerate(fns):
            if i == len(fns) - 1:
                self.q["tensor"].append(lambda e, fn=fn, h=h: fn(e).then_inc(h, 1))
            else:
                self.q["tensor"].append(fn)
        self._commit("tensor", "pe", v, reads, writes)
        return v

    def dma(self, queue, out_ap, in_ap, reads=(), writes=(), sem=None, **kw):
        self._deps(queue, reads, writes)
        if sem is None:
            lanes = self.dma_lanes[queue]
            s = lanes[self.dma_rot[queue] % self.NLANE]
            self.dma_rot[queue] += 1
        else:
            s = sem
        self._wait(queue, s, self.cnt[s])
        self.cnt[s] += 16
        v = self.cnt[s]
        h = self.sems[s]
        self.q[queue].append(
            lambda e, o=out_ap, i=in_ap, h=h, kw=kw: e.dma_start(
                out=(o() if callable(o) else o), in_=(i() if callable(i) else i),
                **kw).then_inc(h, 16))
        self._commit(queue, s, v, reads, writes)
        return v

    def barrier(self):
        sems = ["dve", "act", "pe", "pool"]
        for lanes in self.dma_lanes.values():
            sems += lanes
        for eng in self.ENGINES:
            for s in sems:
                self._wait(eng, s, self.cnt[s])

    def finalize(self):
        nc = self.nc
        blk_cm = nc.Block()
        block = blk_cm.__enter__()
        for eng in self.ENGINES:
            closures = self.q[eng]
            getattr(block, eng)(lambda e, cs=closures: [fn(e) for fn in cs] and None)
        blk_cm.__exit__(None, None, None)
        self.stack.close()


def build(tok=4096, th=1024, layers=NLAYER, ffn=True, dumps=(), stop_after=99, ph4_mode=0):
    TOK, TH, NL = tok, th, layers
    NFD = TOK // FD
    NTH = TOK // TH
    HALF = TOK // 2
    HTOK = TOK // 2
    NFD_F = HTOK // FD

    nc = bass.Bass("TRN2", target_bir_lowering=False)

    def din(name, shape):
        return nc.dram_tensor(name, list(shape), F32, kind="ExternalInput").ap()

    xT = din("xT", (D_MODEL, TOK))
    wi_t = din("wi_t", (NLAYER, EBLK, DBLK, 128, 128))
    wo_t = din("wo_t", (NLAYER, DBLK, CBLK, 128, 128))
    wx_t = din("wx_t", (NLAYER, CBLK, 128, 96))
    wdt_t = din("wdt_t", (NLAYER, CBLK, 65, 128))
    conv_w = din("conv_w", (NLAYER, 128, CBLK, 4))
    conv_b = din("conv_b", (NLAYER, 128, CBLK))
    a_log = din("a_log", (NLAYER, 128, CBLK, NST))
    d_par = din("d_par", (NLAYER, 128, CBLK))
    up_t = din("up_t", (NEXP, 4, DBLK, 128, 128))
    gp_t = din("gp_t", (NEXP, 4, DBLK, 128, 128))
    dn_t = din("dn_t", (NEXP, DBLK, 4, 128, 128))
    shg_t = din("shg_t", (DBLK, DBLK, 128, 128))
    shu_t = din("shu_t", (DBLK, DBLK, 128, 128))
    shd_t = din("shd_t", (2, DBLK, 4, 128, 128))
    rw_t = din("rw_t", (DBLK, 128, NEXP))
    rb_in = din("rb", (NEXP, 1))
    ones_in = din("ones128", (128, 128))
    sel16_in = din("sel16", (NST, NST * 128))
    tok_base = nc.dram_tensor("tok_base", [1, 1], mybir.dt.int32, kind="ExternalInput").ap()
    ident_in = din("ident128", (128, 128))

    outT = nc.dram_tensor("outT", [D_MODEL, HTOK], F32, kind="ExternalOutput").ap()

    def dscr(name, shape, dt):
        return nc.dram_tensor(name, list(shape), dt).ap()

    x_str = dscr("x_str", (DBLK, 128, TOK), F32)
    u_scr = dscr("u_scr", (CBLK, 128, TOK), BF16)
    sz_scr = dscr("sz_scr", (CBLK, 128, TOK), BF16)
    dl_scr = dscr("dl_scr", (CBLK, 128, TOK), BF16)
    g_scr = dscr("g_scr", (CBLK, 128, TOK), BF16)
    xc_scr = dscr("xc_scr", (DBLK, 128, HTOK), F32)
    flat_scr = dscr("flat_scr", (DBLK, 128, HTOK), BF16)
    facc_scr = dscr("facc_scr", (DBLK, 128, HTOK), F32)
    scr_map = dict(x_str=x_str, u_scr=u_scr, sz_scr=sz_scr, dl_scr=dl_scr,
                   g_scr=g_scr, xc_scr=xc_scr, flat_scr=flat_scr, facc_scr=facc_scr)

    p = Prog(nc)
    S_out = Buf("S_out")
    S_xstr, S_u, S_sz, S_dl, S_g, S_xc, S_flat, S_facc = (
        Buf("S_xstr"), Buf("S_u"), Buf("S_sz"), Buf("S_dl"), Buf("S_g"),
        Buf("S_xc"), Buf("S_flat"), Buf("S_facc"))
    scr_buf = dict(x_str=S_xstr, u_scr=S_u, sz_scr=S_sz, dl_scr=S_dl,
                   g_scr=S_g, xc_scr=S_xc, flat_scr=S_flat, facc_scr=S_facc)

    ones_f, B_ones_f = p.sb(None, "ones_f", [128, 128], F32)
    ones_b, B_ones_b = p.sb(None, "ones_b", [128, 128], BF16)
    sel16, B_sel16 = p.sb(None, "sel16", [NST, NST * 128], BF16)
    ident, B_ident = p.sb(None, "ident", [128, 128], F32)
    rb_sb, B_rb = p.sb(None, "rb_sb", [NST, 1], F32)
    negA, B_negA = p.sb(None, "negA", [128, CBLK * NST], F32)
    convw_sb, B_convw = p.sb(None, "convw_sb", [128, CBLK * 4], F32)
    convb_sb, B_convb = p.sb(None, "convb_sb", [128, CBLK], F32)
    dpar_sb, B_dpar = p.sb(None, "dpar_sb", [128, CBLK], F32)
    dt_sb, B_dt = p.sb(None, "dt_sb", [65, TOK], BF16)
    Brow, B_Brow = p.sb(None, "Brow", [NST, TOK], BF16)
    Crow, B_Crow = p.sb(None, "Crow", [NST, TOK], BF16)
    hcar, B_hcar = p.sb(None, "hcar", [128, CBLK * NST], F32)
    comb_sb, B_comb = p.sb(None, "comb_sb", [NST, HTOK], BF16)
    tb_sb, B_tb = p.sb(None, "tb_sb", [1, 1], mybir.dt.int32)
    tok_cell = {}

    p.dma("sync", ones_f[:], ones_in, writes=[B_ones_f])
    p.dma("gpsimd", ones_b[:], ones_in, writes=[B_ones_b])
    p.dma("gpsimd", sel16[:], sel16_in, writes=[B_sel16])
    p.dma("sync", ident[:], ident_in, writes=[B_ident])
    p.dma("sync", rb_sb[:], rb_in, writes=[B_rb])

    def ln_ws(sc, tag):
        d = {}
        d["ps_s"], d["Bps_s"] = p.ps(sc, f"{tag}_ps", [128, FD])
        d["ps_q"], d["Bps_q"] = p.ps(sc, f"{tag}_pq", [128, FD])
        d["sq"], d["Bsq"] = p.sb(sc, f"{tag}_sq", [128, FD], F32)
        d["vec"], d["Bvec"] = p.sb(sc, f"{tag}_vec", [1, 4 * FD], F32)
        d["bc"], d["Bbc"] = p.ps(sc, f"{tag}_bc", [128, 2 * FD])
        d["outt"], d["Bout"] = p.sb(sc, f"{tag}_out", [128, FD], F32)
        return d

    def feature_ln(ws, tiles, bufs, w, consume):
        ps_s, Bps_s = ws["ps_s"], ws["Bps_s"]
        ps_q, Bps_q = ws["ps_q"], ws["Bps_q"]
        sq, Bsq = ws["sq"], ws["Bsq"]
        vec, Bvec = ws["vec"], ws["Bvec"]
        bc, Bbc = ws["bc"], ws["Bbc"]
        outt, Bout = ws["outt"], ws["Bout"]

        p.mmg([lambda e, i=i, ap=tiles[i]: e.matmul(ps_s[0:1, 0:w], ones_f[:, 0:1], ap[:, 0:w],
                                                    start=(i == 0), stop=(i == DBLK - 1))
               for i in range(DBLK)], reads=[B_ones_f] + bufs, writes=[Bps_s])
        for i in range(DBLK):
            p.op("scalar", lambda e, ap=tiles[i]: e.activation(sq[:, 0:w], ap[:, 0:w], AF.Square),
                 reads=[bufs[i]], writes=[Bsq])
            p.mmg([lambda e, i=i: e.matmul(ps_q[0:1, 0:w], ones_f[:, 0:1], sq[:, 0:w],
                                           start=(i == 0), stop=(i == DBLK - 1))],
                  reads=[B_ones_f, Bsq], writes=[Bps_q])
        inv = 1.0 / D_MODEL
        vm = vec[0:1, 0 * FD:0 * FD + w]
        vv = vec[0:1, 1 * FD:1 * FD + w]
        vr = vec[0:1, 2 * FD:2 * FD + w]
        vmr = vec[0:1, 3 * FD:3 * FD + w]
        p.op("vector", lambda e: e.tensor_scalar_mul(vm, ps_s[0:1, 0:w], inv),
             reads=[Bps_s], writes=[Bvec])
        p.op("vector", lambda e: e.tensor_scalar_mul(vv, ps_q[0:1, 0:w], inv),
             reads=[Bps_q], writes=[Bvec])
        p.op("vector", lambda e: e.tensor_tensor(vr, vm, vm, ALU.mult),
             reads=[Bvec], writes=[Bvec])
        p.op("vector", lambda e: e.tensor_tensor(vv, vv, vr, ALU.subtract),
             reads=[Bvec], writes=[Bvec])
        p.op("vector", lambda e: e.tensor_scalar_add(vv, vv, EPS),
             reads=[Bvec], writes=[Bvec])
        p.op("scalar", lambda e: e.activation(vr, vv, AF.Sqrt),
             reads=[Bvec], writes=[Bvec])
        p.op("vector", lambda e: e.reciprocal(vr, vr),
             reads=[Bvec], writes=[Bvec])
        p.op("vector", lambda e: e.tensor_tensor(vmr, vm, vr, ALU.mult),
             reads=[Bvec], writes=[Bvec])
        p.mmg([lambda e: e.matmul(bc[:, 0:w], ones_f[0:1, :], vr, start=True, stop=True),
               lambda e: e.matmul(bc[:, FD:FD + w], ones_f[0:1, :], vmr, start=True, stop=True)],
              reads=[B_ones_f, Bvec], writes=[Bbc])
        for i in range(DBLK):
            p.op("vector", lambda e, ap=tiles[i]: e.tensor_tensor(outt[:, 0:w], ap[:, 0:w], bc[:, 0:w], ALU.mult),
                 reads=[bufs[i], Bbc], writes=[Bout])
            p.op("vector", lambda e: e.tensor_tensor(outt[:, 0:w], outt[:, 0:w], bc[:, FD:FD + w], ALU.subtract),
                 reads=[Bbc], writes=[Bout])
            consume(i, outt, Bout)

    with p.scope() as sc:
        tiles, bufs = [], []
        for i in range(DBLK):
            t, b = p.sb(sc, f"ln1_x{i}", [128, FD], F32)
            tiles.append(t); bufs.append(b)
        ws1 = ln_ws(sc, "ln1")
        for fd in range(NFD):
            for i in range(DBLK):
                p.dma("sync", tiles[i][:, 0:FD], xT[i * 128:(i + 1) * 128, fd * FD:(fd + 1) * FD],
                      writes=[bufs[i]])

            def consume(i, outt, Bout, fd=fd):
                p.dma("sync", x_str[i, :, fd * FD:(fd + 1) * FD], outt[:, 0:FD],
                      reads=[Bout], writes=[S_xstr])
            feature_ln(ws1, tiles, bufs, FD, consume)
        p.barrier()

    def do_layer(lay):
        p.dma("sync", negA[:], a_log[lay].rearrange("p b n -> p (b n)"), writes=[B_negA])
        p.dma("sync", convw_sb[:], conv_w[lay].rearrange("p b k -> p (b k)"), writes=[B_convw])
        p.dma("sync", convb_sb[:], conv_b[lay], writes=[B_convb])
        p.dma("sync", dpar_sb[:], d_par[lay], writes=[B_dpar])
        p.op("scalar", lambda e: e.activation(negA[:], negA[:], AF.Exp), reads=[B_negA], writes=[B_negA])
        p.op("vector", lambda e: e.tensor_scalar_mul(negA[:], negA[:], -1.0), reads=[B_negA], writes=[B_negA])
        p.op("vector", lambda e: e.memset(hcar[:], 0.0), writes=[B_hcar])
        p.op("vector", lambda e: e.memset(dt_sb[64:65, :], 1.0), writes=[B_dt])

        with p.scope() as sc:
            x_bf, B_xbf = [], []
            for i in range(DBLK):
                t, b = p.sb(sc, f"xbf{i}", [128, TOK], BF16)
                x_bf.append(t); B_xbf.append(b)
                p.dma("gpsimd", t[:], x_str[i], reads=[S_xstr], writes=[b])
            wsl, B_wsl = [], []
            for i in range(2):
                t, b = p.sb(sc, f"wisl{i}", [128, DBLK * 128], BF16)
                wsl.append(t); B_wsl.append(b)
            xin, B_xin = [], []
            for i in range(2):
                t, b = p.sb(sc, f"xin{i}", [128, 4 + TOK], BF16)
                xin.append(t); B_xin.append(b)
            cacc, B_cacc = [], []
            for i in range(2):
                t, b = p.sb(sc, f"cacc{i}", [128, TOK], F32)
                cacc.append(t); B_cacc.append(b)
            u_t, B_u = [], []
            for i in range(2):
                t, b = p.sb(sc, f"u_t{i}", [128, TOK], BF16)
                u_t.append(t); B_u.append(b)
            zt, B_zt = [], []
            for i in range(2):
                t, b = p.sb(sc, f"zt{i}", [128, FD], BF16)
                zt.append(t); B_zt.append(b)
            pbank, B_pbank = [], []
            for i in range(4):
                t, b = p.ps(sc, f"ph1b{i}", [128, FD])
                pbank.append(t); B_pbank.append(b)

            for m in range(EBLK):
                sl = m % 2
                p.dma("gpsimd", wsl[sl][:].rearrange("p (k j) -> p k j", k=DBLK), wi_t[lay, m].rearrange("k p j -> p k j"),
                      writes=[B_wsl[sl]])
                is_x = m < CBLK
                dst, B_dst = (xin[m % 2], B_xin[m % 2]) if is_x else (None, None)
                for fd in range(NFD):
                    bk = 2 * sl + (fd % 2)
                    p.mmg([
                        (lambda e, kb=kb, sl=sl, bk=bk, fd=fd:
                         e.matmul(pbank[bk][:, 0:FD], wsl[sl][:, kb * 128:(kb + 1) * 128],
                                  x_bf[kb][:, fd * FD:(fd + 1) * FD],
                                  start=(kb == 0), stop=(kb == DBLK - 1)))
                        for kb in range(DBLK)],
                        reads=[B_wsl[sl]] + B_xbf, writes=[B_pbank[bk]])
                    if is_x:
                        p.op("scalar", lambda e, bk=bk, d=dst, fd=fd:
                             e.copy(d[:, 4 + fd * FD:4 + (fd + 1) * FD], pbank[bk][:, 0:FD]),
                             reads=[B_pbank[bk]], writes=[B_dst])
                    else:
                        blkz = m - CBLK
                        p.op("scalar", lambda e, bk=bk, z=zt[fd % 2]:
                             e.activation(z[:, 0:FD], pbank[bk][:, 0:FD], AF.Silu),
                             reads=[B_pbank[bk]], writes=[B_zt[fd % 2]])
                        p.dma("sync", sz_scr[blkz, :, fd * FD:(fd + 1) * FD], zt[fd % 2][:, 0:FD],
                              reads=[B_zt[fd % 2]], writes=[S_sz])
                if is_x:
                    blk = m
                    acc, B_acc = cacc[m % 2], B_cacc[m % 2]
                    p.op("gpsimd", lambda e, d=dst: e.memset(d[:, 0:4], 0.0),
                         writes=[B_dst])
                    p.op("vector", lambda e, d=dst, acc=acc, c=blk * 4:
                         e.tensor_scalar_mul(acc[:, 0:TOK], d[:, 1:1 + TOK], convw_sb[:, c:c + 1]),
                         reads=[B_dst, B_convw], writes=[B_acc])
                    for k in (1, 2, 3):
                        p.op("vector", lambda e, d=dst, acc=acc, c=blk * 4 + k, k=k:
                             e.scalar_tensor_tensor(acc[:, 0:TOK], d[:, 1 + k:1 + k + TOK],
                                                    convw_sb[:, c:c + 1], acc[:, 0:TOK],
                                                    ALU.mult, ALU.add),
                             reads=[B_dst, B_convw], writes=[B_acc])
                    ut, B_ut = u_t[m % 2], B_u[m % 2]
                    p.op("scalar", lambda e, acc=acc, ut=ut, blk=blk:
                         e.activation(ut[:, 0:TOK], acc[:, 0:TOK], AF.Silu, bias=convb_sb[:, blk:blk + 1]),
                         reads=[B_acc, B_convb], writes=[B_ut])
                    p.dma("sync", u_scr[blk], ut[:, 0:TOK], reads=[B_ut], writes=[S_u])
            p.barrier()

        if stop_after < 2:
            return
        if stop_after < 4:
            return
        with p.scope() as sc:
            sc3 = sc.enter_context(p.scope())
            wxs, B_wxs = p.sb(sc3, "wxs", [128, CBLK * 96], BF16)
            p.dma("gpsimd", wxs[:].rearrange("p (b m) -> p b m", b=CBLK),
                  wx_t[lay].rearrange("b p m -> p b m"), writes=[B_wxs])
            ur, B_ur = [], []
            for i in range(4):
                t, b = p.sb(sc3, f"ph3u{i}", [128, HALF // 2], BF16)
                ur.append(t); B_ur.append(b)
            psA, B_psA = p.ps(sc3, "ph3A", [128, HALF])
            psC, B_psC = p.ps(sc3, "ph3C", [128, HALF])
            QU = HALF // 2
            for h in range(2):
                t0_ = h * HALF
                for q_ in range(2):
                  for kb in range(CBLK):
                    p.dma("gpsimd", ur[kb % 4][:], u_scr[kb, :, t0_ + q_ * QU:t0_ + (q_ + 1) * QU],
                          reads=[S_u], writes=[B_ur[kb % 4]])
                    for fd in range(QU // FD):
                        fda = q_ * (QU // FD) + fd
                        p.mmg([(lambda e, kb=kb, fd=fd, fda=fda, sl=kb % 4:
                                e.matmul(psA[0:80, fda * FD:(fda + 1) * FD],
                                         wxs[:, kb * 96:kb * 96 + 80],
                                         ur[sl][:, fd * FD:(fd + 1) * FD],
                                         start=(kb == 0), stop=(kb == CBLK - 1)))],
                              reads=[B_wxs, B_ur[kb % 4]], writes=[B_psA])
                        p.mmg([(lambda e, kb=kb, fd=fd, fda=fda, sl=kb % 4:
                                e.matmul(psC[0:16, fda * FD:(fda + 1) * FD],
                                         wxs[:, kb * 96 + 80:kb * 96 + 96],
                                         ur[sl][:, fd * FD:(fd + 1) * FD],
                                         start=(kb == 0), stop=(kb == CBLK - 1)))],
                              reads=[B_wxs, B_ur[kb % 4]], writes=[B_psC])
                p.op("vector", lambda e, t0_=t0_: e.tensor_copy(dt_sb[0:64, t0_:t0_ + HALF], psA[0:64, :]),
                     reads=[B_psA], writes=[B_dt])
                p.op("vector", lambda e, t0_=t0_: e.tensor_copy(Brow[0:16, t0_:t0_ + HALF], psA[64:80, :]),
                     reads=[B_psA], writes=[B_Brow])
                p.op("scalar", lambda e, t0_=t0_: e.copy(Crow[0:16, t0_:t0_ + HALF], psC[0:16, :]),
                     reads=[B_psC], writes=[B_Crow])
            sc3.close()
            Bb, B_Bb = p.sb(sc, "Bb", [128, NST * TH], BF16)
            Cb, B_Cb = p.sb(sc, "Cb", [128, NST * TH], BF16)
            wdts, B_wdts = p.sb(sc, "wdts", [65, CBLK * 128], BF16)
            p.dma("gpsimd", wdts[:].rearrange("r (b j) -> r b j", b=CBLK),
                  wdt_t[lay].rearrange("b r j -> r b j"), writes=[B_wdts])
            spa2, B_spa2 = [], []
            spb2, B_spb2 = [], []
            for i in range(2):
                t, b = p.sb(sc, f"spa{i}", [128, TH], BF16); spa2.append(t); B_spa2.append(b)
                t, b = p.sb(sc, f"spb{i}", [128, TH], BF16); spb2.append(t); B_spb2.append(b)
            psD, B_psD = [], []
            for i in range(2):
                t, b = p.ps(sc, f"scand{i}", [128, TH])
                psD.append(t); B_psD.append(b)

            psB, B_psB = [], []
            for i in range(2):
                t, b = p.ps(sc, f"scanps{i}", [128, TH])
                psB.append(t); B_psB.append(b)
            dl_b, B_dlb = [], []
            u_b, B_ub = [], []
            sz_b, B_szb = [], []
            for i in range(2):
                t, b = p.sb(sc, f"sdl{i}", [128, TH], BF16); dl_b.append(t); B_dlb.append(b)
                t, b = p.sb(sc, f"su{i}", [128, TH], BF16); u_b.append(t); B_ub.append(b)
                t, b = p.sb(sc, f"ssz{i}", [128, TH], BF16); sz_b.append(t); B_szb.append(b)
            du_b, B_dub = p.sb(sc, "sdu", [128, TH], BF16)
            a_q, B_aq = [], []
            for i in range(2):
                t, b = p.sb(sc, f"aq{i}", [128, 4 * TH], BF16); a_q.append(t); B_aq.append(b)
            b_q, B_bq = p.sb(sc, "bq", [128, 4 * TH], BF16)
            h_q, B_hq = p.sb(sc, "hq", [128, 4 * TH], BF16)
            car4, B_car4 = p.sb(sc, "car4", [128, 4], F32)
            qA, B_qA = p.sb(sc, "qA", [128, 4 * TH], BF16)
            qB, B_qB = p.sb(sc, "qB", [128, 4 * TH], BF16)
            yac, B_yac = p.sb(sc, "yac", [128, TH], F32)
            y2, B_y2 = p.sb(sc, "y2", [128, TH], BF16)
            g_t, B_gt = [], []
            for i in range(2):
                t, b = p.sb(sc, f"gt{i}", [128, TH], BF16); g_t.append(t); B_gt.append(b)
            wosl, B_wosl = [], []
            for i in range(2):
                t, b = p.sb(sc, f"wosl{i}", [128, CBLK * 128], BF16); wosl.append(t); B_wosl.append(b)
            gr, B_gr = [], []
            for kb in range(8):
                t, b = p.sb(sc, f"gr{kb}", [128, FD], BF16); gr.append(t); B_gr.append(b)
            xres, B_xres = [], []
            for i in range(2):
                t, b = p.sb(sc, f"xres{i}", [128, FD], F32); xres.append(t); B_xres.append(b)

            for t_i in range(NTH):
                t0 = t_i * TH
                for n in range(NST):
                    sl = n % 2
                    p.mmg([(lambda e, n=n, sub=sub, sl=sl:
                            e.matmul(psB[sl][:, sub * FD:(sub + 1) * FD],
                                     sel16[:, n * 128:(n + 1) * 128],
                                     Brow[:, t0 + sub * FD:t0 + (sub + 1) * FD],
                                     start=True, stop=True))
                           for sub in range(TH // FD)],
                          reads=[B_sel16, B_Brow, B_Crow], writes=[B_psB[sl]])
                    p.op("vector", lambda e, n=n, sl=sl:
                         e.tensor_copy(Bb[:, n * TH:(n + 1) * TH], psB[sl][:]),
                         reads=[B_psB[sl]], writes=[B_Bb])
                    p.mmg([(lambda e, n=n, sub=sub, sl=sl:
                            e.matmul(psB[sl][:, sub * FD:(sub + 1) * FD],
                                     sel16[:, n * 128:(n + 1) * 128],
                                     Crow[:, t0 + sub * FD:t0 + (sub + 1) * FD],
                                     start=True, stop=True))
                           for sub in range(TH // FD)],
                          reads=[B_sel16, B_Crow], writes=[B_psB[sl]])
                    p.op("scalar", lambda e, n=n, sl=sl:
                         e.copy(Cb[:, n * TH:(n + 1) * TH], psB[sl][:]),
                         reads=[B_psB[sl]], writes=[B_Cb])

                def delta_chain(blk, t0=t0):
                    sl = blk % 2
                    p.mmg([(lambda e, blk=blk, sub=sub, sl=sl:
                            e.matmul(psD[sl][:, sub * FD:(sub + 1) * FD],
                                     wdts[:, blk * 128:(blk + 1) * 128],
                                     dt_sb[:, t0 + sub * FD:t0 + (sub + 1) * FD],
                                     start=True, stop=True))
                           for sub in range(TH // FD)],
                          reads=[B_wdts, B_dt, B_Brow, B_Crow], writes=[B_psD[sl]])
                    p.op("scalar", lambda e, sl=sl: e.activation(spb2[sl][:], psD[sl][:], AF.Relu),
                         reads=[B_psD[sl]], writes=[B_spb2[sl]])
                    p.op("scalar", lambda e, sl=sl: e.activation(spa2[sl][:], psD[sl][:], AF.Abs),
                         reads=[B_psD[sl]], writes=[B_spa2[sl]])
                    p.op("scalar", lambda e, sl=sl: e.activation(spa2[sl][:], spa2[sl][:], AF.Exp, scale=-1.0),
                         reads=[B_spa2[sl]], writes=[B_spa2[sl]])
                    p.op("scalar", lambda e, sl=sl: e.activation(spa2[sl][:], spa2[sl][:], AF.Ln, bias=1.0),
                         reads=[B_spa2[sl]], writes=[B_spa2[sl]])
                    p.op("gpsimd", lambda e, sl=sl: e.tensor_tensor(dl_b[sl][:], spa2[sl][:], spb2[sl][:], ALU.add),
                         reads=[B_spa2[sl], B_spb2[sl]], writes=[B_dlb[sl]])
                    p.dma("sync", u_b[sl][:], u_scr[blk, :, t0:t0 + TH], reads=[S_u], writes=[B_ub[sl]])
                    p.dma("sync", sz_b[sl][:], sz_scr[blk, :, t0:t0 + TH], reads=[S_sz], writes=[B_szb[sl]])

                delta_chain(0)
                for blk in range(CBLK):
                    sl = blk % 2
                    if blk + 1 < CBLK:
                        delta_chain(blk + 1)
                    p.op("vector", lambda e, sl=sl: e.tensor_tensor(du_b[:], dl_b[sl][:], u_b[sl][:], ALU.mult),
                         reads=[B_dlb[sl], B_ub[sl]], writes=[B_dub])
                    for g in range(4):
                        asl = g % 2
                        at, B_at = a_q[asl], B_aq[asl]
                        for j in range(4):
                            n = g * 4 + j
                            p.op("scalar", lambda e, sl=sl, j=j, n=n, at=at, blk=blk:
                                 e.activation(at[:, j * TH:(j + 1) * TH], dl_b[sl][:], AF.Exp,
                                              scale=negA[:, blk * NST + n:blk * NST + n + 1]),
                                 reads=[B_dlb[sl], B_negA], writes=[B_at])
                        dub = du_b[:].rearrange("p (o t) -> p o t", o=1).broadcast_to([128, 4, TH])
                        p.op("vector", lambda e, g=g, dub=dub:
                             e.tensor_tensor(b_q[:].rearrange("p (o t) -> p o t", o=4), dub,
                                             Bb[:, g * 4 * TH:(g + 1) * 4 * TH].rearrange("p (o t) -> p o t", o=4),
                                             ALU.mult),
                             reads=[B_dub, B_Bb], writes=[B_bq])
                        a3 = at[:].rearrange("p (o t) -> p o t", o=4)
                        b3 = b_q[:].rearrange("p (o t) -> p o t", o=4)
                        p.op("vector", lambda e, a3=a3, blk=blk, g=g:
                             e.tensor_tensor(car4[:], a3[:, :, 0:1].squeeze(2),
                                             hcar[:, blk * NST + g * 4:blk * NST + g * 4 + 4], ALU.mult),
                             reads=[B_at, B_hcar], writes=[B_car4])
                        p.op("vector", lambda e, b3=b3:
                             e.tensor_tensor(b3[:, :, 0:1].squeeze(2), b3[:, :, 0:1].squeeze(2),
                                             car4[:], ALU.add),
                             reads=[B_car4], writes=[B_bq])
                        p.op("vector", lambda e, a3=a3: e.memset(a3[:, :, 0:1], 0.0),
                             writes=[B_at])
                        p.op("vector", lambda e, at=at:
                             e.tensor_tensor_scan(h_q[:], at[:], b_q[:], 0.0, ALU.mult, ALU.add),
                             reads=[B_at, B_bq], writes=[B_hq])
                        h3 = h_q[:].rearrange("p (o t) -> p o t", o=4)
                        p.op("vector", lambda e, h3=h3, blk=blk, g=g:
                             e.tensor_copy(hcar[:, blk * NST + g * 4:blk * NST + g * 4 + 4],
                                           h3[:, :, TH - 1:TH].squeeze(2)),
                             reads=[B_hq], writes=[B_hcar])
                        qdst, B_qdst = (qA, B_qA) if g == 0 else (qB, B_qB)
                        p.op("vector", lambda e, g=g, qdst=qdst:
                             e.tensor_tensor(qdst[:], h_q[:],
                                             Cb[:, g * 4 * TH:(g + 1) * 4 * TH], ALU.mult),
                             reads=[B_hq, B_Cb], writes=[B_qdst])
                        if g in (1, 2):
                            p.op("vector", lambda e: e.tensor_tensor(qA[:], qA[:], qB[:], ALU.add),
                                 reads=[B_qB], writes=[B_qA])
                    p.op("vector", lambda e: e.tensor_tensor(qB[:], qA[:], qB[:], ALU.add),
                         reads=[B_qA], writes=[B_qB])
                    p.op("vector", lambda e: e.tensor_tensor(qB[:, 0:2 * TH], qB[:, 0:2 * TH],
                                                             qB[:, 2 * TH:4 * TH], ALU.add),
                         reads=[B_qB], writes=[B_qB])
                    p.op("vector", lambda e: e.tensor_tensor(yac[:], qB[:, 0:TH], qB[:, TH:2 * TH], ALU.add),
                         reads=[B_qB], writes=[B_yac])
                    p.op("vector", lambda e, sl=sl, blk=blk:
                         e.scalar_tensor_tensor(y2[:], u_b[sl][:], dpar_sb[:, blk:blk + 1], yac[:],
                                                ALU.mult, ALU.add),
                         reads=[B_ub[sl], B_dpar, B_yac], writes=[B_y2])
                    gsl = blk % 2
                    p.op("gpsimd", lambda e, sl=sl, gsl=gsl:
                         e.tensor_tensor(g_t[gsl][:], y2[:], sz_b[sl][:], ALU.mult),
                         reads=[B_y2, B_szb[sl]], writes=[B_gt[gsl]])
                    p.dma("sync", g_scr[blk, :, t0:t0 + TH], g_t[gsl][:],
                          reads=[B_gt[gsl]], writes=[S_g])
                for fdl in range(TH // FD):
                    pslots = [psB[0][:, 0:FD], psB[0][:, FD:2 * FD],
                              psB[1][:, 0:FD], psB[1][:, FD:2 * FD],
                              psD[0][:, 0:FD], psD[0][:, FD:2 * FD],
                              psD[1][:, 0:FD], psD[1][:, FD:2 * FD]]
                    pbufs = [B_psB[0], B_psB[0], B_psB[1], B_psB[1],
                             B_psD[0], B_psD[0], B_psD[1], B_psD[1]]
                    for kh in range(2):
                        for kb in range(8):
                            p.dma("sync", gr[kb][:],
                                  g_scr[kh * 8 + kb, :, t0 + fdl * FD:t0 + (fdl + 1) * FD],
                                  reads=[S_g], writes=[B_gr[kb]])
                        for m in range(DBLK):
                            wsl_ = m % 2
                            p.dma("gpsimd",
                                  wosl[wsl_][:, 0:8 * 128].rearrange("p (k j) -> p k j", k=8),
                                  wo_t[lay, m, kh * 8:(kh + 1) * 8].rearrange("k p j -> p k j"),
                                  writes=[B_wosl[wsl_]])
                            p.mmg([(lambda e, kb=kb, m=m, wsl_=wsl_, kh=kh:
                                    e.matmul(pslots[m], wosl[wsl_][:, kb * 128:(kb + 1) * 128],
                                             gr[kb][:, 0:FD], start=(kh == 0 and kb == 0),
                                             stop=(kh == 1 and kb == 7)))
                                   for kb in range(8)],
                                  reads=[B_wosl[wsl_]] + B_gr, writes=[pbufs[m]])
                    for m in range(DBLK):
                        xsl = m % 2
                        p.op("scalar", lambda e, xsl=xsl, m=m:
                             e.copy(xres[xsl][:], pslots[m]),
                             reads=[pbufs[m]], writes=[B_xres[xsl]])
                        p.dma("gpsimd", x_str[m, :, t0 + fdl * FD:t0 + (fdl + 1) * FD], xres[xsl][:],
                              reads=[B_xres[xsl]], writes=[S_xstr], accum_op=ALU.add)
            p.barrier()

        if stop_after < 5:
            return

    if stop_after >= 1:
        for lay in range(NL):
            do_layer(lay)

    def do_combine1():
        p.dma("sync", tb_sb[:], tok_base, writes=[B_tb])
        p._deps("sync", [B_tb], [])
        def _load_tok(e):
            r = e.register("tokreg").__enter__()
            e.reg_load(r, tb_sb[0:1, 0:1])
            sv = e.snap(r, min_val=0, max_val=HTOK)
            for fd in range(NFD_F):
                tok_cell[fd] = e.snap(sv + fd * FD, min_val=fd * FD, max_val=HTOK + fd * FD)
        p.q["sync"].append(_load_tok)
        with p.scope() as sc:
            mt_all, B_mta = p.sb(sc, "c1m", [128, DBLK * FD], F32)
            x0_all, B_x0a = p.sb(sc, "c1xorig", [128, DBLK * FD], F32)
            mt = [mt_all[:, i * FD:(i + 1) * FD] for i in range(DBLK)]
            B_mt = [B_mta] * DBLK
            xct, B_xct = [], []
            for i in range(DBLK):
                t, bb = p.sb(sc, f"c1x{i}", [128, FD], F32); xct.append(t); B_xct.append(bb)
            msk, B_msk = p.sb(sc, "c1k", [128, FD], F32)
            fbt, B_fbt = p.sb(sc, "c1fb", [128, FD], BF16)
            wsa = ln_ws(sc, "c1a")
            wsb = ln_ws(sc, "c1b")
            for fd in range(NFD_F):
                p.dma("sync", mt_all[:].rearrange("p (b t) -> p b t", b=DBLK),
                      (lambda fd=fd: x_str.rearrange("b p t -> p b t")[:, :, bass.ds(tok_cell[fd], FD)]),
                      reads=[S_xstr], writes=[B_mta])
                p.dma("sync", x0_all[:].rearrange("p (b t) -> p b t", b=DBLK),
                      (lambda fd=fd: xT.rearrange("(b p) t -> p b t", p=128)[:, :, bass.ds(tok_cell[fd], FD)]),
                      writes=[B_x0a])
                for i in range(DBLK):
                    p.op("vector", lambda e, i=i: e.tensor_tensor(msk[:], mt[i], mt[i], ALU.is_equal),
                         reads=[B_mta], writes=[B_msk])
                    p.op("vector", lambda e, i=i: e.tensor_scalar_max(mt[i], mt[i], -FMAX),
                         reads=[B_msk], writes=[B_mta])
                    p.op("vector", lambda e, i=i: e.tensor_scalar_min(mt[i], mt[i], FMAX),
                         reads=[], writes=[B_mta])
                    p.op("vector", lambda e, i=i: e.tensor_tensor(mt[i], mt[i], msk[:], ALU.mult),
                         reads=[B_msk], writes=[B_mta])

                def consume1(i, outt, Bout, fd=fd):
                    p.op("vector", lambda e, i=i: e.tensor_tensor(xct[i][:], x0_all[:, i * FD:(i + 1) * FD],
                                                                  outt[:, 0:FD], ALU.add),
                         reads=[B_x0a, Bout], writes=[B_xct[i]])
                    p.dma("sync", xc_scr[i, :, fd * FD:(fd + 1) * FD], xct[i][:],
                          reads=[B_xct[i]], writes=[S_xc])
                feature_ln(wsa, mt, B_mt, FD, consume1)

                def consume2(i, outt, Bout, fd=fd):
                    p.op("vector", lambda e: e.tensor_copy(fbt[:], outt[:, 0:FD]),
                         reads=[Bout], writes=[B_fbt])
                    p.dma("sync", flat_scr[i, :, fd * FD:(fd + 1) * FD], fbt[:],
                          reads=[B_fbt], writes=[S_flat])
                feature_ln(wsb, xct, B_xct, FD, consume2)
            p.barrier()

    if stop_after >= 6:
        do_combine1()

    if ffn and stop_after >= 7:
        def do_router():
          with p.scope() as sc:
            rws, B_rws = p.sb(sc, "rws", [128, DBLK * NST], BF16)
            p.dma("gpsimd", rws[:].rearrange("p (b m) -> p b m", b=DBLK), rw_t.rearrange("b p m -> p b m"), writes=[B_rws])
            fb, B_fb = [], []
            for i in range(DBLK):
                t, b = p.sb(sc, f"rfb{i}", [128, FD], BF16)
                fb.append(t); B_fb.append(b)
            psl, B_psl = p.ps(sc, "rps", [128, FD])
            st_, B_st = p.sb(sc, "rst", [NST, FD], F32)
            pst, B_pst = p.ps(sc, "rpt", [128, NST])
            stt, B_stt = p.sb(sc, "rstt", [128, NST], F32)
            m1, B_m1 = p.sb(sc, "rm1", [128, 1], F32)
            m2, B_m2 = p.sb(sc, "rm2", [128, 1], F32)
            mk1, B_mk1 = p.sb(sc, "rmk1", [128, NST], F32)
            mk2, B_mk2 = p.sb(sc, "rmk2", [128, NST], F32)
            w1, B_w1 = p.sb(sc, "rw1", [128, 1], F32)
            w2, B_w2 = p.sb(sc, "rw2", [128, 1], F32)
            ct, B_ct = p.sb(sc, "rct", [128, NST], F32)
            cps, B_cps = p.ps(sc, "rcps", [128, 128])
            for fd in range(NFD_F):
                for i in range(DBLK):
                    p.dma("sync", fb[i][:], flat_scr[i, :, fd * FD:(fd + 1) * FD],
                          reads=[S_flat], writes=[B_fb[i]])
                p.mmg([(lambda e, kb=kb:
                        e.matmul(psl[0:NST, 0:FD], rws[:, kb * NST:(kb + 1) * NST], fb[kb][:, 0:FD],
                                 start=(kb == 0), stop=(kb == DBLK - 1)))
                       for kb in range(DBLK)],
                      reads=[B_rws] + B_fb, writes=[B_psl])
                p.op("vector", lambda e: e.tensor_scalar_add(st_[:, 0:FD], psl[0:NST, 0:FD], rb_sb[0:NST, 0:1]),
                     reads=[B_psl, B_rb], writes=[B_st])
                for ch in range(FD // 128):
                    p.mmg([lambda e, ch=ch: e.transpose(pst[:, 0:NST], st_[0:NST, ch * 128:(ch + 1) * 128], ident[0:NST, 0:NST])],
                          reads=[B_st, B_ident], writes=[B_pst])
                    p.op("vector", lambda e: e.tensor_copy(stt[:], pst[:, 0:NST]),
                         reads=[B_pst], writes=[B_stt])
                    p.op("vector", lambda e: e.tensor_reduce(m1[:], stt[:], AX.X, ALU.max),
                         reads=[B_stt], writes=[B_m1])
                    p.op("vector", lambda e: e.tensor_single_scalar(mk1[:], stt[:], m1[:, 0:1], ALU.is_equal),
                         reads=[B_stt, B_m1], writes=[B_mk1])
                    p.op("vector", lambda e: e.scalar_tensor_tensor(mk2[:], mk1[:], -1e30, stt[:], ALU.mult, ALU.add),
                         reads=[B_mk1, B_stt], writes=[B_mk2])
                    p.op("vector", lambda e: e.tensor_reduce(m2[:], mk2[:], AX.X, ALU.max),
                         reads=[B_mk2], writes=[B_m2])
                    p.op("vector", lambda e: e.tensor_single_scalar(mk2[:], mk2[:], m2[:, 0:1], ALU.is_equal),
                         reads=[B_m2], writes=[B_mk2])
                    p.op("vector", lambda e: e.tensor_tensor(w1[:], m1[:], m2[:], ALU.subtract),
                         reads=[B_m1, B_m2], writes=[B_w1])
                    p.op("scalar", lambda e: e.activation(w1[:], w1[:], AF.Sigmoid),
                         reads=[B_w1], writes=[B_w1])
                    p.op("vector", lambda e: e.tensor_scalar(w2[:], w1[:], -1.0, 1.0, ALU.mult, ALU.add),
                         reads=[B_w1], writes=[B_w2])
                    p.op("vector", lambda e: e.tensor_scalar_mul(ct[:], mk1[:], w1[:, 0:1]),
                         reads=[B_mk1, B_w1], writes=[B_ct])
                    p.op("vector", lambda e: e.scalar_tensor_tensor(ct[:], mk2[:], w2[:, 0:1], ct[:], ALU.mult, ALU.add),
                         reads=[B_mk2, B_w2], writes=[B_ct])
                    p.mmg([lambda e: e.transpose(cps[0:NST, 0:128], ct[:], ident[:])],
                          reads=[B_ct, B_ident], writes=[B_cps])
                    p.op("vector", lambda e, fd=fd, ch=ch:
                         e.tensor_copy(comb_sb[0:NST, fd * FD + ch * 128:fd * FD + (ch + 1) * 128], cps[0:NST, 0:128]),
                         reads=[B_cps], writes=[B_comb])
            p.barrier()

        if stop_after >= 7:
            do_router()

        def do_experts():
          with p.scope() as sc:
            fb, B_fb = [], []
            for i in range(DBLK):
                t, b = p.sb(sc, f"efb{i}", [128, HTOK], BF16)
                fb.append(t); B_fb.append(b)
                p.dma("gpsimd", t[:], flat_scr[i], reads=[S_flat], writes=[b])
            upsl, B_upsl = p.sb(sc, "upsl", [128, 4 * DBLK * 128], BF16)
            gpsl, B_gpsl = p.sb(sc, "gpsl", [128, 4 * DBLK * 128], BF16)
            dnsl, B_dnsl = p.sb(sc, "dnsl", [128, DBLK * 4 * 128], BF16)

            hid, B_hid = [], []
            for m in range(4):
                t, b = p.sb(sc, f"hid{m}", [128, HTOK], BF16)
                hid.append(t); B_hid.append(b)
            psu, B_psu = [], []
            for i in range(4):
                t, b = p.ps(sc, f"eps{i}", [128, FD])
                psu.append(t); B_psu.append(b)
            sgt, B_sgt = p.sb(sc, "sgt", [128, FD], BF16)
            cbp, B_cbp = p.ps(sc, "cbp", [128, FD])
            ot, B_ot = [], []
            for i in range(2):
                t, b = p.sb(sc, f"eot{i}", [128, FD], F32)
                ot.append(t); B_ot.append(b)

            for k in range(NEXP + 2):
                shared = (k >= NEXP)
                s = k - NEXP
                if shared:
                    p.dma("gpsimd", gpsl[:].rearrange("p (m k j) -> p m k j", m=4, k=DBLK),
                          shg_t[4 * s:4 * s + 4].rearrange("m k p j -> p m k j"), writes=[B_gpsl])
                    p.dma("gpsimd", upsl[:].rearrange("p (m k j) -> p m k j", m=4, k=DBLK),
                          shu_t[4 * s:4 * s + 4].rearrange("m k p j -> p m k j"), writes=[B_upsl])
                    p.dma("gpsimd", dnsl[:].rearrange("p (m k j) -> p m k j", m=DBLK, k=4),
                          shd_t[s].rearrange("m k p j -> p m k j"), writes=[B_dnsl])
                else:
                    p.dma("gpsimd", gpsl[:].rearrange("p (m k j) -> p m k j", m=4, k=DBLK),
                          gp_t[k].rearrange("m k p j -> p m k j"), writes=[B_gpsl])
                    p.dma("gpsimd", upsl[:].rearrange("p (m k j) -> p m k j", m=4, k=DBLK),
                          up_t[k].rearrange("m k p j -> p m k j"), writes=[B_upsl])
                    p.dma("gpsimd", dnsl[:].rearrange("p (m k j) -> p m k j", m=DBLK, k=4),
                          dn_t[k].rearrange("m k p j -> p m k j"), writes=[B_dnsl])

                for m in range(4):
                    for fd in range(NFD_F):
                        pg, Bpg = (psu[0], B_psu[0]) if fd % 2 == 0 else (psu[1], B_psu[1])
                        pu, Bpu = (psu[2], B_psu[2]) if fd % 2 == 0 else (psu[3], B_psu[3])
                        p.mmg([(lambda e, kb=kb, m=m, fd=fd, pg=pg:
                                e.matmul(pg[:, 0:FD], gpsl[:, (m * DBLK + kb) * 128:(m * DBLK + kb + 1) * 128],
                                         fb[kb][:, fd * FD:(fd + 1) * FD],
                                         start=(kb == 0), stop=(kb == DBLK - 1)))
                               for kb in range(DBLK)],
                              reads=[B_gpsl] + B_fb, writes=[Bpg])
                        p.mmg([(lambda e, kb=kb, m=m, fd=fd, pu=pu:
                                e.matmul(pu[:, 0:FD], upsl[:, (m * DBLK + kb) * 128:(m * DBLK + kb + 1) * 128],
                                         fb[kb][:, fd * FD:(fd + 1) * FD],
                                         start=(kb == 0), stop=(kb == DBLK - 1)))
                               for kb in range(DBLK)],
                              reads=[B_upsl] + B_fb, writes=[Bpu])
                        p.op("scalar", lambda e, pg=pg: e.activation(sgt[:], pg[:, 0:FD], AF.Silu),
                             reads=[Bpg], writes=[B_sgt])
                        p.op("vector", lambda e, m=m, fd=fd, pu=pu:
                             e.tensor_tensor(hid[m][:, fd * FD:(fd + 1) * FD], sgt[:], pu[:, 0:FD], ALU.mult),
                             reads=[B_sgt, Bpu], writes=[B_hid[m]])
                if not shared:
                    for fd in range(NFD_F):
                        p.mmg([lambda e, fd=fd, k=k:
                               e.matmul(cbp[:, 0:FD], sel16[:, k * 128:(k + 1) * 128],
                                        comb_sb[:, fd * FD:(fd + 1) * FD],
                                        start=True, stop=True)],
                              reads=[B_sel16, B_comb], writes=[B_cbp])
                        for m in range(4):
                            p.op("vector", lambda e, m=m, fd=fd:
                                 e.tensor_tensor(hid[m][:, fd * FD:(fd + 1) * FD],
                                                 hid[m][:, fd * FD:(fd + 1) * FD], cbp[:, 0:FD], ALU.mult),
                                 reads=[B_cbp], writes=[B_hid[m]])
                for m2 in range(DBLK):
                    for fd in range(NFD_F):
                        po, Bpo = (psu[0], B_psu[0]) if fd % 2 == 0 else (psu[1], B_psu[1])
                        p.mmg([(lambda e, kb=kb, m2=m2, fd=fd, po=po:
                                e.matmul(po[:, 0:FD], dnsl[:, (m2 * 4 + kb) * 128:(m2 * 4 + kb + 1) * 128],
                                         hid[kb][:, fd * FD:(fd + 1) * FD],
                                         start=(kb == 0), stop=(kb == 3)))
                               for kb in range(4)],
                              reads=[B_dnsl] + B_hid, writes=[Bpo])
                        osl = fd % 2
                        p.op("vector", lambda e, po=po, osl=osl: e.tensor_copy(ot[osl][:], po[:, 0:FD]),
                             reads=[Bpo], writes=[B_ot[osl]])
                        if k == 0:
                            p.dma("gpsimd", facc_scr[m2, :, fd * FD:(fd + 1) * FD], ot[osl][:],
                                  reads=[B_ot[osl]], writes=[S_facc])
                        else:
                            p.dma("gpsimd", facc_scr[m2, :, fd * FD:(fd + 1) * FD], ot[osl][:],
                                  reads=[B_ot[osl]], writes=[S_facc], accum_op=ALU.add)
            p.barrier()

        do_experts()

    def do_combine2():
      with p.scope() as sc:
        ft, B_ft = [], []
        for i in range(DBLK):
            t, b = p.sb(sc, f"c2f{i}", [128, FD], F32)
            ft.append(t); B_ft.append(b)
        xct, B_xct2 = p.sb(sc, "c2x", [128, FD], F32)
        ws2 = ln_ws(sc, "c2")
        for fd in range(NFD_F):
            for i in range(DBLK):
                src = facc_scr if ffn else xc_scr
                sbf = S_facc if ffn else S_xc
                p.dma("sync", ft[i][:], src[i, :, fd * FD:(fd + 1) * FD],
                      reads=[sbf], writes=[B_ft[i]])

            def consume3(i, outt, Bout, fd=fd):
                p.dma("sync", xct[:], xc_scr[i, :, fd * FD:(fd + 1) * FD],
                      reads=[S_xc], writes=[B_xct2])
                p.op("vector", lambda e: e.tensor_tensor(xct[:], xct[:], outt[:, 0:FD], ALU.add),
                     reads=[Bout], writes=[B_xct2])
                p.dma("sync", outT[i * 128:(i + 1) * 128, fd * FD:(fd + 1) * FD], xct[:],
                      reads=[B_xct2], writes=[S_out])
            feature_ln(ws2, ft, B_ft, FD, consume3)

    if stop_after >= 9:
        do_combine2()

    for nmm in dumps:
        src = scr_map[nmm]
        shp = list(src.shape)
        dt = src.dtype
        dst = nc.dram_tensor(f"dump_{nmm}", shp, dt, kind="ExternalOutput").ap()
        p.barrier()
        p.dma("sync", dst, src, reads=[scr_buf[nmm]], writes=[S_out])

    p.barrier()

    p.finalize()
    return nc


_CACHE = {}


def kernel(**inputs):
    host = prepare_host_inputs(inputs)
    nc = build()
    htok = 2048
    in_maps = []
    for core in range(8):
        b, h = core // 2, core % 2
        m = {k: v for k, v in host.items() if k != "xT"}
        m["xT"] = host["xT"][b]
        m["tok_base"] = np.full((1, 1), h * htok, np.int32)
        in_maps.append(m)
    res = run_bass_kernel_spmd(nc, in_maps, core_ids=list(range(8)))
    full = np.empty((4, 4096, 1024), np.float32)
    for core in range(8):
        b, h = core // 2, core % 2
        full[b, h * htok:(h + 1) * htok, :] = np.asarray(res.results[core]["outT"]).T
    return full


# version 16 (final)
# speedup vs baseline: 1.0402x; 1.0140x over previous
import numpy as np
from contextlib import ExitStack

import concourse.bass as bass
import concourse.mybir as mybir
from concourse.bass_utils import run_bass_kernel_spmd

F32 = mybir.dt.float32
BF16 = mybir.dt.bfloat16
AF = mybir.ActivationFunctionType
ALU = mybir.AluOpType
AX = mybir.AxisListType

D_MODEL = 1024
D_INNER = 2048
E2 = 4096
NST = 16
NLAYER = 8
NEXP = 16
EPS = 1e-5

DBLK = D_MODEL // 128
CBLK = D_INNER // 128
EBLK = E2 // 128
FD = 512
FMAX = 3.4028234663852886e38


def _tile_lhsT(wT):
    K, M = wT.shape
    return np.ascontiguousarray(
        wT.reshape(K // 128, 128, M // 128, 128).transpose(2, 0, 1, 3))


def prepare_host_inputs(ins):
    f = np.float32
    o = {}
    x = np.asarray(ins["L0_attn_x"], f)
    o["xT"] = np.ascontiguousarray(x.transpose(0, 2, 1))

    wi = np.asarray(ins["L0_attn_W_in"], f)
    o["wi_t"] = np.stack([_tile_lhsT(np.ascontiguousarray(wi[l].T)) for l in range(NLAYER)])
    wo = np.asarray(ins["L0_attn_W_out"], f)
    o["wo_t"] = np.stack([_tile_lhsT(np.ascontiguousarray(wo[l].T)) for l in range(NLAYER)])
    wx = np.asarray(ins["L0_attn_W_x"], f)
    o["wx_t"] = np.ascontiguousarray(
        wx.transpose(0, 2, 1).reshape(NLAYER, CBLK, 128, 96))
    wdt = np.asarray(ins["L0_attn_W_dt"], f)
    bdt = np.asarray(ins["L0_attn_b_dt"], f)
    rows = np.concatenate([wdt.transpose(0, 2, 1), bdt[:, None, :]], axis=1)
    o["wdt_t"] = np.ascontiguousarray(
        rows.reshape(NLAYER, 65, CBLK, 128).transpose(0, 2, 1, 3))
    o["conv_w"] = np.ascontiguousarray(
        np.asarray(ins["L0_attn_conv_w"], f).reshape(NLAYER, CBLK, 128, 4).transpose(0, 2, 1, 3))
    o["conv_b"] = np.ascontiguousarray(
        np.asarray(ins["L0_attn_conv_b"], f).reshape(NLAYER, CBLK, 128).transpose(0, 2, 1))
    o["a_log"] = np.ascontiguousarray(
        np.asarray(ins["L0_attn_A_log"], f).reshape(NLAYER, CBLK, 128, NST).transpose(0, 2, 1, 3))
    o["d_par"] = np.ascontiguousarray(
        np.asarray(ins["L0_attn_D_par"], f).reshape(NLAYER, CBLK, 128).transpose(0, 2, 1))

    up = np.asarray(ins["L0_ffn_up_proj"], f)
    o["up_t"] = np.stack([_tile_lhsT(up[k]) for k in range(NEXP)])
    gp = np.asarray(ins["L0_ffn_gate_proj"], f)
    o["gp_t"] = np.stack([_tile_lhsT(gp[k]) for k in range(NEXP)])
    dn = np.asarray(ins["L0_ffn_down_proj"], f)
    o["dn_t"] = np.stack([_tile_lhsT(dn[k]) for k in range(NEXP)])
    o["shg_t"] = _tile_lhsT(np.ascontiguousarray(np.asarray(ins["L0_ffn_sh_gate"], f).T))
    o["shu_t"] = _tile_lhsT(np.ascontiguousarray(np.asarray(ins["L0_ffn_sh_up"], f).T))
    shd_full = _tile_lhsT(np.ascontiguousarray(np.asarray(ins["L0_ffn_sh_down"], f).T))
    o["shd_t"] = np.ascontiguousarray(np.stack([shd_full[:, 0:4], shd_full[:, 4:8]]))
    rw = np.asarray(ins["L0_ffn_router_w"], f)
    o["rw_t"] = np.ascontiguousarray(rw.T.reshape(DBLK, 128, NEXP))
    o["rb"] = np.ascontiguousarray(np.asarray(ins["L0_ffn_router_bias"], f).reshape(NEXP, 1))

    o["ones128"] = np.ones((128, 128), f)
    o["sel16"] = np.ascontiguousarray(np.kron(np.eye(16, dtype=f), np.ones((1, 128), f)))
    o["ident128"] = np.eye(128, dtype=f)
    return o


class Buf:
    __slots__ = ("w", "readers", "name")

    def __init__(self, name):
        self.name = name
        self.w = None
        self.readers = []


class Prog:
    ENGINES = ("sync", "gpsimd", "tensor", "vector", "scalar")
    DEF_SEM = dict(tensor="pe", vector="dve", scalar="act", gpsimd="pool")
    NLANE = 4

    def __init__(self, nc):
        self.nc = nc
        self.q = {e: [] for e in self.ENGINES}
        self.stack = ExitStack()
        self.sems = {}
        self.cnt = {}
        self.hiwater = {e: {} for e in self.ENGINES}
        names = ["pe", "act", "dve", "pool", "st"]
        self.dma_lanes = {}
        for qn, tag in (("sync", "s"), ("gpsimd", "g"), ("scalar", "c")):
            lanes = [f"dma_{tag}{i}" for i in range(self.NLANE)]
            self.dma_lanes[qn] = lanes
            names += lanes
        for s in names:
            self.sems[s] = self.stack.enter_context(nc.semaphore(s))
            self.cnt[s] = 0
        self.dma_rot = {qn: 0 for qn in self.dma_lanes}

    _serial = 0

    def _uniq(self, name):
        Prog._serial += 1
        return f"{name}_{Prog._serial}"

    def scope(self):
        return ExitStack()

    def sb(self, scope, name, shape, dt):
        name = self._uniq(name)
        t = (scope or self.stack).enter_context(self.nc.sbuf_tensor(name, shape, dt))
        return t, Buf(name)

    def ps(self, scope, name, shape):
        name = self._uniq(name)
        t = (scope or self.stack).enter_context(self.nc.psum_tensor(name, shape, F32))
        return t, Buf(name)

    def _wait(self, eng, sem, val):
        if val <= 0:
            return
        hw = self.hiwater[eng]
        if hw.get(sem, 0) >= val:
            return
        hw[sem] = val
        h = self.sems[sem]
        self.q[eng].append(lambda e, h=h, v=val: e.wait_ge(h, v))

    def _deps(self, eng, reads, writes):
        for b in reads:
            if b.w:
                self._wait(eng, b.w[0], b.w[1])
        for b in writes:
            if b.w:
                self._wait(eng, b.w[0], b.w[1])
            for (s, v, src) in b.readers:
                self._wait(eng, s, v)

    def _commit(self, eng, sem, val, reads, writes):
        for b in reads:
            b.readers.append((sem, val, eng))
        for b in writes:
            b.w = (sem, val, eng)
            b.readers = []

    def op(self, eng, fn, reads=(), writes=()):
        self._deps(eng, reads, writes)
        sem = self.DEF_SEM[eng]
        self.cnt[sem] += 1
        v = self.cnt[sem]
        h = self.sems[sem]
        self.q[eng].append(lambda e, fn=fn, h=h: fn(e).then_inc(h, 1))
        self._commit(eng, sem, v, reads, writes)
        return v

    def mmg(self, fns, reads=(), writes=()):
        self._deps("tensor", reads, writes)
        self.cnt["pe"] += 1
        v = self.cnt["pe"]
        h = self.sems["pe"]
        for i, fn in enumerate(fns):
            if i == len(fns) - 1:
                self.q["tensor"].append(lambda e, fn=fn, h=h: fn(e).then_inc(h, 1))
            else:
                self.q["tensor"].append(fn)
        self._commit("tensor", "pe", v, reads, writes)
        return v

    def dma(self, queue, out_ap, in_ap, reads=(), writes=(), sem=None, **kw):
        self._deps(queue, reads, writes)
        if sem is None:
            lanes = self.dma_lanes[queue]
            s = lanes[self.dma_rot[queue] % self.NLANE]
            self.dma_rot[queue] += 1
        else:
            s = sem
        self._wait(queue, s, self.cnt[s])
        self.cnt[s] += 16
        v = self.cnt[s]
        h = self.sems[s]
        self.q[queue].append(
            lambda e, o=out_ap, i=in_ap, h=h, kw=kw: e.dma_start(
                out=(o() if callable(o) else o), in_=(i() if callable(i) else i),
                **kw).then_inc(h, 16))
        self._commit(queue, s, v, reads, writes)
        return v

    def barrier(self):
        sems = ["dve", "act", "pe", "pool"]
        for lanes in self.dma_lanes.values():
            sems += lanes
        for eng in self.ENGINES:
            for s in sems:
                self._wait(eng, s, self.cnt[s])

    def finalize(self):
        nc = self.nc
        blk_cm = nc.Block()
        block = blk_cm.__enter__()
        for eng in self.ENGINES:
            closures = self.q[eng]
            getattr(block, eng)(lambda e, cs=closures: [fn(e) for fn in cs] and None)
        blk_cm.__exit__(None, None, None)
        self.stack.close()


def build(tok=4096, th=1024, layers=NLAYER, ffn=True, dumps=(), stop_after=99, ph4_mode=0):
    TOK, TH, NL = tok, th, layers
    NFD = TOK // FD
    NTH = TOK // TH
    HALF = TOK // 2
    HTOK = TOK // 2
    NFD_F = HTOK // FD

    nc = bass.Bass("TRN2", target_bir_lowering=False)

    def din(name, shape):
        return nc.dram_tensor(name, list(shape), F32, kind="ExternalInput").ap()

    xT = din("xT", (D_MODEL, TOK))
    wi_t = din("wi_t", (NLAYER, EBLK, DBLK, 128, 128))
    wo_t = din("wo_t", (NLAYER, DBLK, CBLK, 128, 128))
    wx_t = din("wx_t", (NLAYER, CBLK, 128, 96))
    wdt_t = din("wdt_t", (NLAYER, CBLK, 65, 128))
    conv_w = din("conv_w", (NLAYER, 128, CBLK, 4))
    conv_b = din("conv_b", (NLAYER, 128, CBLK))
    a_log = din("a_log", (NLAYER, 128, CBLK, NST))
    d_par = din("d_par", (NLAYER, 128, CBLK))
    up_t = din("up_t", (NEXP, 4, DBLK, 128, 128))
    gp_t = din("gp_t", (NEXP, 4, DBLK, 128, 128))
    dn_t = din("dn_t", (NEXP, DBLK, 4, 128, 128))
    shg_t = din("shg_t", (DBLK, DBLK, 128, 128))
    shu_t = din("shu_t", (DBLK, DBLK, 128, 128))
    shd_t = din("shd_t", (2, DBLK, 4, 128, 128))
    rw_t = din("rw_t", (DBLK, 128, NEXP))
    rb_in = din("rb", (NEXP, 1))
    ones_in = din("ones128", (128, 128))
    sel16_in = din("sel16", (NST, NST * 128))
    tok_base = nc.dram_tensor("tok_base", [1, 1], mybir.dt.int32, kind="ExternalInput").ap()
    ident_in = din("ident128", (128, 128))

    outT = nc.dram_tensor("outT", [D_MODEL, HTOK], F32, kind="ExternalOutput").ap()

    def dscr(name, shape, dt):
        return nc.dram_tensor(name, list(shape), dt).ap()

    x_str = dscr("x_str", (DBLK, 128, TOK), F32)
    u_scr = dscr("u_scr", (CBLK, 128, TOK), BF16)
    sz_scr = dscr("sz_scr", (CBLK, 128, TOK), BF16)
    dl_scr = dscr("dl_scr", (CBLK, 128, TOK), BF16)
    g_scr = dscr("g_scr", (CBLK, 128, TOK), BF16)
    xc_scr = dscr("xc_scr", (DBLK, 128, HTOK), F32)
    flat_scr = dscr("flat_scr", (DBLK, 128, HTOK), BF16)
    facc_scr = dscr("facc_scr", (DBLK, 128, HTOK), F32)
    scr_map = dict(x_str=x_str, u_scr=u_scr, sz_scr=sz_scr, dl_scr=dl_scr,
                   g_scr=g_scr, xc_scr=xc_scr, flat_scr=flat_scr, facc_scr=facc_scr)

    p = Prog(nc)
    S_out = Buf("S_out")
    S_xstr, S_u, S_sz, S_dl, S_g, S_xc, S_flat, S_facc = (
        Buf("S_xstr"), Buf("S_u"), Buf("S_sz"), Buf("S_dl"), Buf("S_g"),
        Buf("S_xc"), Buf("S_flat"), Buf("S_facc"))
    scr_buf = dict(x_str=S_xstr, u_scr=S_u, sz_scr=S_sz, dl_scr=S_dl,
                   g_scr=S_g, xc_scr=S_xc, flat_scr=S_flat, facc_scr=S_facc)

    ones_f, B_ones_f = p.sb(None, "ones_f", [128, 128], F32)
    ones_b, B_ones_b = p.sb(None, "ones_b", [128, 128], BF16)
    sel16, B_sel16 = p.sb(None, "sel16", [NST, NST * 128], BF16)
    ident, B_ident = p.sb(None, "ident", [128, 128], F32)
    rb_sb, B_rb = p.sb(None, "rb_sb", [NST, 1], F32)
    negA, B_negA = p.sb(None, "negA", [128, CBLK * NST], F32)
    convw_sb, B_convw = p.sb(None, "convw_sb", [128, CBLK * 4], F32)
    convb_sb, B_convb = p.sb(None, "convb_sb", [128, CBLK], F32)
    dpar_sb, B_dpar = p.sb(None, "dpar_sb", [128, CBLK], F32)
    dt_sb, B_dt = p.sb(None, "dt_sb", [65, TOK], BF16)
    Brow, B_Brow = p.sb(None, "Brow", [NST, TOK], BF16)
    Crow, B_Crow = p.sb(None, "Crow", [NST, TOK], BF16)
    hcar, B_hcar = p.sb(None, "hcar", [128, CBLK * NST], F32)
    comb_sb, B_comb = p.sb(None, "comb_sb", [NST, HTOK], BF16)
    tb_sb, B_tb = p.sb(None, "tb_sb", [1, 1], mybir.dt.int32)
    tok_cell = {}

    p.dma("sync", ones_f[:], ones_in, writes=[B_ones_f])
    p.dma("gpsimd", ones_b[:], ones_in, writes=[B_ones_b])
    p.dma("gpsimd", sel16[:], sel16_in, writes=[B_sel16])
    p.dma("sync", ident[:], ident_in, writes=[B_ident])
    p.dma("sync", rb_sb[:], rb_in, writes=[B_rb])

    def ln_ws(sc, tag):
        d = {}
        d["ps_s"], d["Bps_s"] = p.ps(sc, f"{tag}_ps", [128, FD])
        d["ps_q"], d["Bps_q"] = p.ps(sc, f"{tag}_pq", [128, FD])
        d["sq"], d["Bsq"] = p.sb(sc, f"{tag}_sq", [128, FD], F32)
        d["vec"], d["Bvec"] = p.sb(sc, f"{tag}_vec", [1, 4 * FD], F32)
        d["bc"], d["Bbc"] = p.ps(sc, f"{tag}_bc", [128, 2 * FD])
        d["outt"], d["Bout"] = p.sb(sc, f"{tag}_out", [128, FD], F32)
        return d

    def feature_ln(ws, tiles, bufs, w, consume):
        ps_s, Bps_s = ws["ps_s"], ws["Bps_s"]
        ps_q, Bps_q = ws["ps_q"], ws["Bps_q"]
        sq, Bsq = ws["sq"], ws["Bsq"]
        vec, Bvec = ws["vec"], ws["Bvec"]
        bc, Bbc = ws["bc"], ws["Bbc"]
        outt, Bout = ws["outt"], ws["Bout"]

        p.mmg([lambda e, i=i, ap=tiles[i]: e.matmul(ps_s[0:1, 0:w], ones_f[:, 0:1], ap[:, 0:w],
                                                    start=(i == 0), stop=(i == DBLK - 1))
               for i in range(DBLK)], reads=[B_ones_f] + bufs, writes=[Bps_s])
        for i in range(DBLK):
            p.op("scalar", lambda e, ap=tiles[i]: e.activation(sq[:, 0:w], ap[:, 0:w], AF.Square),
                 reads=[bufs[i]], writes=[Bsq])
            p.mmg([lambda e, i=i: e.matmul(ps_q[0:1, 0:w], ones_f[:, 0:1], sq[:, 0:w],
                                           start=(i == 0), stop=(i == DBLK - 1))],
                  reads=[B_ones_f, Bsq], writes=[Bps_q])
        inv = 1.0 / D_MODEL
        vm = vec[0:1, 0 * FD:0 * FD + w]
        vv = vec[0:1, 1 * FD:1 * FD + w]
        vr = vec[0:1, 2 * FD:2 * FD + w]
        vmr = vec[0:1, 3 * FD:3 * FD + w]
        p.op("vector", lambda e: e.tensor_scalar_mul(vm, ps_s[0:1, 0:w], inv),
             reads=[Bps_s], writes=[Bvec])
        p.op("vector", lambda e: e.tensor_scalar_mul(vv, ps_q[0:1, 0:w], inv),
             reads=[Bps_q], writes=[Bvec])
        p.op("vector", lambda e: e.tensor_tensor(vr, vm, vm, ALU.mult),
             reads=[Bvec], writes=[Bvec])
        p.op("vector", lambda e: e.tensor_tensor(vv, vv, vr, ALU.subtract),
             reads=[Bvec], writes=[Bvec])
        p.op("vector", lambda e: e.tensor_scalar_add(vv, vv, EPS),
             reads=[Bvec], writes=[Bvec])
        p.op("scalar", lambda e: e.activation(vr, vv, AF.Sqrt),
             reads=[Bvec], writes=[Bvec])
        p.op("vector", lambda e: e.reciprocal(vr, vr),
             reads=[Bvec], writes=[Bvec])
        p.op("vector", lambda e: e.tensor_tensor(vmr, vm, vr, ALU.mult),
             reads=[Bvec], writes=[Bvec])
        p.mmg([lambda e: e.matmul(bc[:, 0:w], ones_f[0:1, :], vr, start=True, stop=True),
               lambda e: e.matmul(bc[:, FD:FD + w], ones_f[0:1, :], vmr, start=True, stop=True)],
              reads=[B_ones_f, Bvec], writes=[Bbc])
        for i in range(DBLK):
            p.op("vector", lambda e, ap=tiles[i]: e.tensor_tensor(outt[:, 0:w], ap[:, 0:w], bc[:, 0:w], ALU.mult),
                 reads=[bufs[i], Bbc], writes=[Bout])
            p.op("vector", lambda e: e.tensor_tensor(outt[:, 0:w], outt[:, 0:w], bc[:, FD:FD + w], ALU.subtract),
                 reads=[Bbc], writes=[Bout])
            consume(i, outt, Bout)

    with p.scope() as sc:
        tiles, bufs = [], []
        for i in range(DBLK):
            t, b = p.sb(sc, f"ln1_x{i}", [128, FD], F32)
            tiles.append(t); bufs.append(b)
        ws1 = ln_ws(sc, "ln1")
        for fd in range(NFD):
            for i in range(DBLK):
                p.dma("sync", tiles[i][:, 0:FD], xT[i * 128:(i + 1) * 128, fd * FD:(fd + 1) * FD],
                      writes=[bufs[i]])

            def consume(i, outt, Bout, fd=fd):
                p.dma("sync", x_str[i, :, fd * FD:(fd + 1) * FD], outt[:, 0:FD],
                      reads=[Bout], writes=[S_xstr])
            feature_ln(ws1, tiles, bufs, FD, consume)
        p.barrier()

    def do_layer(lay):
        p.dma("sync", negA[:], a_log[lay].rearrange("p b n -> p (b n)"), writes=[B_negA])
        p.dma("sync", convw_sb[:], conv_w[lay].rearrange("p b k -> p (b k)"), writes=[B_convw])
        p.dma("sync", convb_sb[:], conv_b[lay], writes=[B_convb])
        p.dma("sync", dpar_sb[:], d_par[lay], writes=[B_dpar])
        p.op("scalar", lambda e: e.activation(negA[:], negA[:], AF.Exp), reads=[B_negA], writes=[B_negA])
        p.op("vector", lambda e: e.tensor_scalar_mul(negA[:], negA[:], -1.0), reads=[B_negA], writes=[B_negA])
        p.op("vector", lambda e: e.memset(hcar[:], 0.0), writes=[B_hcar])
        p.op("vector", lambda e: e.memset(dt_sb[64:65, :], 1.0), writes=[B_dt])

        with p.scope() as sc:
            x_bf, B_xbf = [], []
            for i in range(DBLK):
                t, b = p.sb(sc, f"xbf{i}", [128, TOK], BF16)
                x_bf.append(t); B_xbf.append(b)
                p.dma("gpsimd", t[:], x_str[i], reads=[S_xstr], writes=[b])
            wsl, B_wsl = [], []
            for i in range(2):
                t, b = p.sb(sc, f"wisl{i}", [128, DBLK * 128], BF16)
                wsl.append(t); B_wsl.append(b)
            xin, B_xin = [], []
            for i in range(2):
                t, b = p.sb(sc, f"xin{i}", [128, 4 + TOK], BF16)
                xin.append(t); B_xin.append(b)
            cacc, B_cacc = [], []
            for i in range(2):
                t, b = p.sb(sc, f"cacc{i}", [128, TOK], F32)
                cacc.append(t); B_cacc.append(b)
            u_t, B_u = [], []
            for i in range(2):
                t, b = p.sb(sc, f"u_t{i}", [128, TOK], BF16)
                u_t.append(t); B_u.append(b)
            zt, B_zt = [], []
            for i in range(2):
                t, b = p.sb(sc, f"zt{i}", [128, FD], BF16)
                zt.append(t); B_zt.append(b)
            pbank, B_pbank = [], []
            for i in range(4):
                t, b = p.ps(sc, f"ph1b{i}", [128, FD])
                pbank.append(t); B_pbank.append(b)

            for m in range(EBLK):
                sl = m % 2
                p.dma("gpsimd", wsl[sl][:].rearrange("p (k j) -> p k j", k=DBLK), wi_t[lay, m].rearrange("k p j -> p k j"),
                      writes=[B_wsl[sl]])
                is_x = m < CBLK
                dst, B_dst = (xin[m % 2], B_xin[m % 2]) if is_x else (None, None)
                for fd in range(NFD):
                    bk = 2 * sl + (fd % 2)
                    p.mmg([
                        (lambda e, kb=kb, sl=sl, bk=bk, fd=fd:
                         e.matmul(pbank[bk][:, 0:FD], wsl[sl][:, kb * 128:(kb + 1) * 128],
                                  x_bf[kb][:, fd * FD:(fd + 1) * FD],
                                  start=(kb == 0), stop=(kb == DBLK - 1)))
                        for kb in range(DBLK)],
                        reads=[B_wsl[sl]] + B_xbf, writes=[B_pbank[bk]])
                    if is_x:
                        p.op("scalar", lambda e, bk=bk, d=dst, fd=fd:
                             e.copy(d[:, 4 + fd * FD:4 + (fd + 1) * FD], pbank[bk][:, 0:FD]),
                             reads=[B_pbank[bk]], writes=[B_dst])
                    else:
                        blkz = m - CBLK
                        p.op("scalar", lambda e, bk=bk, z=zt[fd % 2]:
                             e.activation(z[:, 0:FD], pbank[bk][:, 0:FD], AF.Silu),
                             reads=[B_pbank[bk]], writes=[B_zt[fd % 2]])
                        p.dma("sync", sz_scr[blkz, :, fd * FD:(fd + 1) * FD], zt[fd % 2][:, 0:FD],
                              reads=[B_zt[fd % 2]], writes=[S_sz])
                if is_x:
                    blk = m
                    acc, B_acc = cacc[m % 2], B_cacc[m % 2]
                    p.op("gpsimd", lambda e, d=dst: e.memset(d[:, 0:4], 0.0),
                         writes=[B_dst])
                    p.op("vector", lambda e, d=dst, acc=acc, c=blk * 4:
                         e.tensor_scalar_mul(acc[:, 0:TOK], d[:, 1:1 + TOK], convw_sb[:, c:c + 1]),
                         reads=[B_dst, B_convw], writes=[B_acc])
                    for k in (1, 2, 3):
                        p.op("vector", lambda e, d=dst, acc=acc, c=blk * 4 + k, k=k:
                             e.scalar_tensor_tensor(acc[:, 0:TOK], d[:, 1 + k:1 + k + TOK],
                                                    convw_sb[:, c:c + 1], acc[:, 0:TOK],
                                                    ALU.mult, ALU.add),
                             reads=[B_dst, B_convw], writes=[B_acc])
                    ut, B_ut = u_t[m % 2], B_u[m % 2]
                    p.op("scalar", lambda e, acc=acc, ut=ut, blk=blk:
                         e.activation(ut[:, 0:TOK], acc[:, 0:TOK], AF.Silu, bias=convb_sb[:, blk:blk + 1]),
                         reads=[B_acc, B_convb], writes=[B_ut])
                    p.dma("sync", u_scr[blk], ut[:, 0:TOK], reads=[B_ut], writes=[S_u])
            p.barrier()

        if stop_after < 2:
            return
        if stop_after < 4:
            return
        with p.scope() as sc:
            sc3 = sc.enter_context(p.scope())
            wxs, B_wxs = p.sb(sc3, "wxs", [128, CBLK * 96], BF16)
            p.dma("gpsimd", wxs[:].rearrange("p (b m) -> p b m", b=CBLK),
                  wx_t[lay].rearrange("b p m -> p b m"), writes=[B_wxs])
            ur, B_ur = [], []
            for i in range(2):
                t, b = p.sb(sc3, f"ph3u{i}", [128, HALF], BF16)
                ur.append(t); B_ur.append(b)
            psA, B_psA = p.ps(sc3, "ph3A", [128, HALF])
            psC, B_psC = p.ps(sc3, "ph3C", [128, HALF])
            QU = HALF // 2
            for h in range(2):
                t0_ = h * HALF
                for q_ in range(2):
                  for kb2 in range(CBLK // 2):
                    sl = kb2 % 2
                    p.dma("gpsimd",
                          ur[sl][:].rearrange("p (b t) -> p b t", b=2),
                          u_scr[2 * kb2:2 * kb2 + 2, :, t0_ + q_ * QU:t0_ + (q_ + 1) * QU].rearrange(
                              "b p t -> p b t"),
                          reads=[S_u], writes=[B_ur[sl]])
                    for j in range(2):
                      kb = 2 * kb2 + j
                      for fd in range(QU // FD):
                        fda = q_ * (QU // FD) + fd
                        p.mmg([(lambda e, kb=kb, j=j, fd=fd, fda=fda, sl=sl:
                                e.matmul(psA[0:80, fda * FD:(fda + 1) * FD],
                                         wxs[:, kb * 96:kb * 96 + 80],
                                         ur[sl][:, j * QU + fd * FD:j * QU + (fd + 1) * FD],
                                         start=(kb == 0), stop=(kb == CBLK - 1)))],
                              reads=[B_wxs, B_ur[sl]], writes=[B_psA])
                        p.mmg([(lambda e, kb=kb, j=j, fd=fd, fda=fda, sl=sl:
                                e.matmul(psC[0:16, fda * FD:(fda + 1) * FD],
                                         wxs[:, kb * 96 + 80:kb * 96 + 96],
                                         ur[sl][:, j * QU + fd * FD:j * QU + (fd + 1) * FD],
                                         start=(kb == 0), stop=(kb == CBLK - 1)))],
                              reads=[B_wxs, B_ur[sl]], writes=[B_psC])
                p.op("vector", lambda e, t0_=t0_: e.tensor_copy(dt_sb[0:64, t0_:t0_ + HALF], psA[0:64, :]),
                     reads=[B_psA], writes=[B_dt])
                p.op("vector", lambda e, t0_=t0_: e.tensor_copy(Brow[0:16, t0_:t0_ + HALF], psA[64:80, :]),
                     reads=[B_psA], writes=[B_Brow])
                p.op("scalar", lambda e, t0_=t0_: e.copy(Crow[0:16, t0_:t0_ + HALF], psC[0:16, :]),
                     reads=[B_psC], writes=[B_Crow])
            sc3.close()
            Bb, B_Bb = p.sb(sc, "Bb", [128, NST * TH], BF16)
            Cb, B_Cb = p.sb(sc, "Cb", [128, NST * TH], BF16)
            wdts, B_wdts = p.sb(sc, "wdts", [65, CBLK * 128], BF16)
            p.dma("gpsimd", wdts[:].rearrange("r (b j) -> r b j", b=CBLK),
                  wdt_t[lay].rearrange("b r j -> r b j"), writes=[B_wdts])
            spa2, B_spa2 = [], []
            spb2, B_spb2 = [], []
            for i in range(2):
                t, b = p.sb(sc, f"spa{i}", [128, TH], BF16); spa2.append(t); B_spa2.append(b)
                t, b = p.sb(sc, f"spb{i}", [128, TH], BF16); spb2.append(t); B_spb2.append(b)
            psD, B_psD = [], []
            for i in range(2):
                t, b = p.ps(sc, f"scand{i}", [128, TH])
                psD.append(t); B_psD.append(b)

            psB, B_psB = [], []
            for i in range(2):
                t, b = p.ps(sc, f"scanps{i}", [128, TH])
                psB.append(t); B_psB.append(b)
            dl_b, B_dlb = [], []
            u_b, B_ub = [], []
            sz_b, B_szb = [], []
            for i in range(2):
                t, b = p.sb(sc, f"sdl{i}", [128, TH], BF16); dl_b.append(t); B_dlb.append(b)
                t, b = p.sb(sc, f"su{i}", [128, TH], BF16); u_b.append(t); B_ub.append(b)
                t, b = p.sb(sc, f"ssz{i}", [128, TH], BF16); sz_b.append(t); B_szb.append(b)
            du_b, B_dub = p.sb(sc, "sdu", [128, TH], BF16)
            a_q, B_aq = [], []
            for i in range(2):
                t, b = p.sb(sc, f"aq{i}", [128, 4 * TH], BF16); a_q.append(t); B_aq.append(b)
            b_q, B_bq = p.sb(sc, "bq", [128, 4 * TH], BF16)
            h_q, B_hq = p.sb(sc, "hq", [128, 4 * TH], BF16)
            car4, B_car4 = p.sb(sc, "car4", [128, 4], F32)
            qA, B_qA = p.sb(sc, "qA", [128, 4 * TH], BF16)
            qB, B_qB = p.sb(sc, "qB", [128, 4 * TH], BF16)
            yac, B_yac = p.sb(sc, "yac", [128, TH], F32)
            y2, B_y2 = p.sb(sc, "y2", [128, TH], BF16)
            g_t, B_gt = [], []
            for i in range(2):
                t, b = p.sb(sc, f"gt{i}", [128, TH], BF16); g_t.append(t); B_gt.append(b)
            wosl, B_wosl = [], []
            for i in range(2):
                t, b = p.sb(sc, f"wosl{i}", [128, CBLK * 128], BF16); wosl.append(t); B_wosl.append(b)
            gr, B_gr = [], []
            for kb in range(8):
                t, b = p.sb(sc, f"gr{kb}", [128, FD], BF16); gr.append(t); B_gr.append(b)
            xres, B_xres = [], []
            for i in range(2):
                t, b = p.sb(sc, f"xres{i}", [128, FD], F32); xres.append(t); B_xres.append(b)

            for t_i in range(NTH):
                t0 = t_i * TH
                bslots = [psB[0], psB[1], psD[0], psD[1]]
                bbufs = [B_psB[0], B_psB[1], B_psD[0], B_psD[1]]
                for n in range(NST):
                    slB = (2 * n) % 4
                    slC = (2 * n + 1) % 4
                    p.mmg([(lambda e, n=n, sub=sub, slB=slB:
                            e.matmul(bslots[slB][:, sub * FD:(sub + 1) * FD],
                                     sel16[:, n * 128:(n + 1) * 128],
                                     Brow[:, t0 + sub * FD:t0 + (sub + 1) * FD],
                                     start=True, stop=True))
                           for sub in range(TH // FD)],
                          reads=[B_sel16, B_Brow, B_Crow], writes=[bbufs[slB]])
                    p.mmg([(lambda e, n=n, sub=sub, slC=slC:
                            e.matmul(bslots[slC][:, sub * FD:(sub + 1) * FD],
                                     sel16[:, n * 128:(n + 1) * 128],
                                     Crow[:, t0 + sub * FD:t0 + (sub + 1) * FD],
                                     start=True, stop=True))
                           for sub in range(TH // FD)],
                          reads=[B_sel16, B_Crow], writes=[bbufs[slC]])
                    p.op("vector", lambda e, n=n, slB=slB:
                         e.tensor_copy(Bb[:, n * TH:(n + 1) * TH], bslots[slB][:]),
                         reads=[bbufs[slB]], writes=[B_Bb])
                    p.op("scalar", lambda e, n=n, slC=slC:
                         e.copy(Cb[:, n * TH:(n + 1) * TH], bslots[slC][:]),
                         reads=[bbufs[slC]], writes=[B_Cb])

                def delta_chain(blk, t0=t0):
                    sl = blk % 2
                    p.mmg([(lambda e, blk=blk, sub=sub, sl=sl:
                            e.matmul(psD[sl][:, sub * FD:(sub + 1) * FD],
                                     wdts[:, blk * 128:(blk + 1) * 128],
                                     dt_sb[:, t0 + sub * FD:t0 + (sub + 1) * FD],
                                     start=True, stop=True))
                           for sub in range(TH // FD)],
                          reads=[B_wdts, B_dt, B_Brow, B_Crow], writes=[B_psD[sl]])
                    p.op("scalar", lambda e, sl=sl: e.activation(spb2[sl][:], psD[sl][:], AF.Relu),
                         reads=[B_psD[sl]], writes=[B_spb2[sl]])
                    p.op("scalar", lambda e, sl=sl: e.activation(spa2[sl][:], psD[sl][:], AF.Abs),
                         reads=[B_psD[sl]], writes=[B_spa2[sl]])
                    p.op("scalar", lambda e, sl=sl: e.activation(spa2[sl][:], spa2[sl][:], AF.Exp, scale=-1.0),
                         reads=[B_spa2[sl]], writes=[B_spa2[sl]])
                    p.op("scalar", lambda e, sl=sl: e.activation(spa2[sl][:], spa2[sl][:], AF.Ln, bias=1.0),
                         reads=[B_spa2[sl]], writes=[B_spa2[sl]])
                    p.op("gpsimd", lambda e, sl=sl: e.tensor_tensor(dl_b[sl][:], spa2[sl][:], spb2[sl][:], ALU.add),
                         reads=[B_spa2[sl], B_spb2[sl]], writes=[B_dlb[sl]])
                    p.dma("sync", u_b[sl][:], u_scr[blk, :, t0:t0 + TH], reads=[S_u], writes=[B_ub[sl]])
                    p.dma("sync", sz_b[sl][:], sz_scr[blk, :, t0:t0 + TH], reads=[S_sz], writes=[B_szb[sl]])

                delta_chain(0)
                for blk in range(CBLK):
                    sl = blk % 2
                    if blk + 1 < CBLK:
                        delta_chain(blk + 1)
                    p.op("vector", lambda e, sl=sl: e.tensor_tensor(du_b[:], dl_b[sl][:], u_b[sl][:], ALU.mult),
                         reads=[B_dlb[sl], B_ub[sl]], writes=[B_dub])
                    for g in range(4):
                        asl = g % 2
                        at, B_at = a_q[asl], B_aq[asl]
                        for j in range(4):
                            n = g * 4 + j
                            p.op("scalar", lambda e, sl=sl, j=j, n=n, at=at, blk=blk:
                                 e.activation(at[:, j * TH:(j + 1) * TH], dl_b[sl][:], AF.Exp,
                                              scale=negA[:, blk * NST + n:blk * NST + n + 1]),
                                 reads=[B_dlb[sl], B_negA], writes=[B_at])
                        dub = du_b[:].rearrange("p (o t) -> p o t", o=1).broadcast_to([128, 4, TH])
                        p.op("vector", lambda e, g=g, dub=dub:
                             e.tensor_tensor(b_q[:].rearrange("p (o t) -> p o t", o=4), dub,
                                             Bb[:, g * 4 * TH:(g + 1) * 4 * TH].rearrange("p (o t) -> p o t", o=4),
                                             ALU.mult),
                             reads=[B_dub, B_Bb], writes=[B_bq])
                        a3 = at[:].rearrange("p (o t) -> p o t", o=4)
                        b3 = b_q[:].rearrange("p (o t) -> p o t", o=4)
                        p.op("vector", lambda e, a3=a3, blk=blk, g=g:
                             e.tensor_tensor(car4[:], a3[:, :, 0:1].squeeze(2),
                                             hcar[:, blk * NST + g * 4:blk * NST + g * 4 + 4], ALU.mult),
                             reads=[B_at, B_hcar], writes=[B_car4])
                        p.op("vector", lambda e, b3=b3:
                             e.tensor_tensor(b3[:, :, 0:1].squeeze(2), b3[:, :, 0:1].squeeze(2),
                                             car4[:], ALU.add),
                             reads=[B_car4], writes=[B_bq])
                        p.op("vector", lambda e, a3=a3: e.memset(a3[:, :, 0:1], 0.0),
                             writes=[B_at])
                        p.op("vector", lambda e, at=at:
                             e.tensor_tensor_scan(h_q[:], at[:], b_q[:], 0.0, ALU.mult, ALU.add),
                             reads=[B_at, B_bq], writes=[B_hq])
                        h3 = h_q[:].rearrange("p (o t) -> p o t", o=4)
                        p.op("vector", lambda e, h3=h3, blk=blk, g=g:
                             e.tensor_copy(hcar[:, blk * NST + g * 4:blk * NST + g * 4 + 4],
                                           h3[:, :, TH - 1:TH].squeeze(2)),
                             reads=[B_hq], writes=[B_hcar])
                        qdst, B_qdst = (qA, B_qA) if g == 0 else (qB, B_qB)
                        p.op("vector", lambda e, g=g, qdst=qdst:
                             e.tensor_tensor(qdst[:], h_q[:],
                                             Cb[:, g * 4 * TH:(g + 1) * 4 * TH], ALU.mult),
                             reads=[B_hq, B_Cb], writes=[B_qdst])
                        if g in (1, 2):
                            p.op("vector", lambda e: e.tensor_tensor(qA[:], qA[:], qB[:], ALU.add),
                                 reads=[B_qB], writes=[B_qA])
                    p.op("vector", lambda e: e.tensor_tensor(qB[:], qA[:], qB[:], ALU.add),
                         reads=[B_qA], writes=[B_qB])
                    p.op("vector", lambda e: e.tensor_tensor(qB[:, 0:2 * TH], qB[:, 0:2 * TH],
                                                             qB[:, 2 * TH:4 * TH], ALU.add),
                         reads=[B_qB], writes=[B_qB])
                    p.op("vector", lambda e: e.tensor_tensor(yac[:], qB[:, 0:TH], qB[:, TH:2 * TH], ALU.add),
                         reads=[B_qB], writes=[B_yac])
                    p.op("vector", lambda e, sl=sl, blk=blk:
                         e.scalar_tensor_tensor(y2[:], u_b[sl][:], dpar_sb[:, blk:blk + 1], yac[:],
                                                ALU.mult, ALU.add),
                         reads=[B_ub[sl], B_dpar, B_yac], writes=[B_y2])
                    gsl = blk % 2
                    p.op("gpsimd", lambda e, sl=sl, gsl=gsl:
                         e.tensor_tensor(g_t[gsl][:], y2[:], sz_b[sl][:], ALU.mult),
                         reads=[B_y2, B_szb[sl]], writes=[B_gt[gsl]])
                    p.dma("sync", g_scr[blk, :, t0:t0 + TH], g_t[gsl][:],
                          reads=[B_gt[gsl]], writes=[S_g])
                for fdl in range(TH // FD):
                    pslots = [psB[0][:, 0:FD], psB[0][:, FD:2 * FD],
                              psB[1][:, 0:FD], psB[1][:, FD:2 * FD],
                              psD[0][:, 0:FD], psD[0][:, FD:2 * FD],
                              psD[1][:, 0:FD], psD[1][:, FD:2 * FD]]
                    pbufs = [B_psB[0], B_psB[0], B_psB[1], B_psB[1],
                             B_psD[0], B_psD[0], B_psD[1], B_psD[1]]
                    for kh in range(2):
                        for kb in range(8):
                            p.dma("sync", gr[kb][:],
                                  g_scr[kh * 8 + kb, :, t0 + fdl * FD:t0 + (fdl + 1) * FD],
                                  reads=[S_g], writes=[B_gr[kb]])
                        for m in range(DBLK):
                            wsl_ = m % 2
                            p.dma("gpsimd",
                                  wosl[wsl_][:, 0:8 * 128].rearrange("p (k j) -> p k j", k=8),
                                  wo_t[lay, m, kh * 8:(kh + 1) * 8].rearrange("k p j -> p k j"),
                                  writes=[B_wosl[wsl_]])
                            p.mmg([(lambda e, kb=kb, m=m, wsl_=wsl_, kh=kh:
                                    e.matmul(pslots[m], wosl[wsl_][:, kb * 128:(kb + 1) * 128],
                                             gr[kb][:, 0:FD], start=(kh == 0 and kb == 0),
                                             stop=(kh == 1 and kb == 7)))
                                   for kb in range(8)],
                                  reads=[B_wosl[wsl_]] + B_gr, writes=[pbufs[m]])
                    for m in range(DBLK):
                        xsl = m % 2
                        p.op("scalar", lambda e, xsl=xsl, m=m:
                             e.copy(xres[xsl][:], pslots[m]),
                             reads=[pbufs[m]], writes=[B_xres[xsl]])
                        p.dma("gpsimd", x_str[m, :, t0 + fdl * FD:t0 + (fdl + 1) * FD], xres[xsl][:],
                              reads=[B_xres[xsl]], writes=[S_xstr], accum_op=ALU.add)
            p.barrier()

        if stop_after < 5:
            return

    if stop_after >= 1:
        for lay in range(NL):
            do_layer(lay)

    def do_combine1():
        p.dma("sync", tb_sb[:], tok_base, writes=[B_tb])
        p._deps("sync", [B_tb], [])
        def _load_tok(e):
            r = e.register("tokreg").__enter__()
            e.reg_load(r, tb_sb[0:1, 0:1])
            sv = e.snap(r, min_val=0, max_val=HTOK)
            for fd in range(NFD_F):
                tok_cell[fd] = e.snap(sv + fd * FD, min_val=fd * FD, max_val=HTOK + fd * FD)
        p.q["sync"].append(_load_tok)
        with p.scope() as sc:
            mt_all, B_mta = p.sb(sc, "c1m", [128, DBLK * FD], F32)
            x0_all, B_x0a = p.sb(sc, "c1xorig", [128, DBLK * FD], F32)
            mt = [mt_all[:, i * FD:(i + 1) * FD] for i in range(DBLK)]
            B_mt = [B_mta] * DBLK
            xct, B_xct = [], []
            for i in range(DBLK):
                t, bb = p.sb(sc, f"c1x{i}", [128, FD], F32); xct.append(t); B_xct.append(bb)
            msk, B_msk = p.sb(sc, "c1k", [128, FD], F32)
            fbt, B_fbt = p.sb(sc, "c1fb", [128, FD], BF16)
            wsa = ln_ws(sc, "c1a")
            wsb = ln_ws(sc, "c1b")
            for fd in range(NFD_F):
                p.dma("sync", mt_all[:].rearrange("p (b t) -> p b t", b=DBLK),
                      (lambda fd=fd: x_str.rearrange("b p t -> p b t")[:, :, bass.ds(tok_cell[fd], FD)]),
                      reads=[S_xstr], writes=[B_mta])
                p.dma("sync", x0_all[:].rearrange("p (b t) -> p b t", b=DBLK),
                      (lambda fd=fd: xT.rearrange("(b p) t -> p b t", p=128)[:, :, bass.ds(tok_cell[fd], FD)]),
                      writes=[B_x0a])
                for i in range(DBLK):
                    p.op("vector", lambda e, i=i: e.tensor_tensor(msk[:], mt[i], mt[i], ALU.is_equal),
                         reads=[B_mta], writes=[B_msk])
                    p.op("vector", lambda e, i=i: e.tensor_scalar_max(mt[i], mt[i], -FMAX),
                         reads=[B_msk], writes=[B_mta])
                    p.op("vector", lambda e, i=i: e.tensor_scalar_min(mt[i], mt[i], FMAX),
                         reads=[], writes=[B_mta])
                    p.op("vector", lambda e, i=i: e.tensor_tensor(mt[i], mt[i], msk[:], ALU.mult),
                         reads=[B_msk], writes=[B_mta])

                def consume1(i, outt, Bout, fd=fd):
                    p.op("vector", lambda e, i=i: e.tensor_tensor(xct[i][:], x0_all[:, i * FD:(i + 1) * FD],
                                                                  outt[:, 0:FD], ALU.add),
                         reads=[B_x0a, Bout], writes=[B_xct[i]])
                    p.dma("sync", xc_scr[i, :, fd * FD:(fd + 1) * FD], xct[i][:],
                          reads=[B_xct[i]], writes=[S_xc])
                feature_ln(wsa, mt, B_mt, FD, consume1)

                def consume2(i, outt, Bout, fd=fd):
                    p.op("vector", lambda e: e.tensor_copy(fbt[:], outt[:, 0:FD]),
                         reads=[Bout], writes=[B_fbt])
                    p.dma("sync", flat_scr[i, :, fd * FD:(fd + 1) * FD], fbt[:],
                          reads=[B_fbt], writes=[S_flat])
                feature_ln(wsb, xct, B_xct, FD, consume2)
            p.barrier()

    if stop_after >= 6:
        do_combine1()

    if ffn and stop_after >= 7:
        def do_router():
          with p.scope() as sc:
            rws, B_rws = p.sb(sc, "rws", [128, DBLK * NST], BF16)
            p.dma("gpsimd", rws[:].rearrange("p (b m) -> p b m", b=DBLK), rw_t.rearrange("b p m -> p b m"), writes=[B_rws])
            fb, B_fb = [], []
            for i in range(DBLK):
                t, b = p.sb(sc, f"rfb{i}", [128, FD], BF16)
                fb.append(t); B_fb.append(b)
            psl, B_psl = p.ps(sc, "rps", [128, FD])
            st_, B_st = p.sb(sc, "rst", [NST, FD], F32)
            pst, B_pst = p.ps(sc, "rpt", [128, NST])
            stt, B_stt = p.sb(sc, "rstt", [128, NST], F32)
            m1, B_m1 = p.sb(sc, "rm1", [128, 1], F32)
            m2, B_m2 = p.sb(sc, "rm2", [128, 1], F32)
            mk1, B_mk1 = p.sb(sc, "rmk1", [128, NST], F32)
            mk2, B_mk2 = p.sb(sc, "rmk2", [128, NST], F32)
            w1, B_w1 = p.sb(sc, "rw1", [128, 1], F32)
            w2, B_w2 = p.sb(sc, "rw2", [128, 1], F32)
            ct, B_ct = p.sb(sc, "rct", [128, NST], F32)
            cps, B_cps = p.ps(sc, "rcps", [128, 128])
            for fd in range(NFD_F):
                for i in range(DBLK):
                    p.dma("sync", fb[i][:], flat_scr[i, :, fd * FD:(fd + 1) * FD],
                          reads=[S_flat], writes=[B_fb[i]])
                p.mmg([(lambda e, kb=kb:
                        e.matmul(psl[0:NST, 0:FD], rws[:, kb * NST:(kb + 1) * NST], fb[kb][:, 0:FD],
                                 start=(kb == 0), stop=(kb == DBLK - 1)))
                       for kb in range(DBLK)],
                      reads=[B_rws] + B_fb, writes=[B_psl])
                p.op("vector", lambda e: e.tensor_scalar_add(st_[:, 0:FD], psl[0:NST, 0:FD], rb_sb[0:NST, 0:1]),
                     reads=[B_psl, B_rb], writes=[B_st])
                for ch in range(FD // 128):
                    p.mmg([lambda e, ch=ch: e.transpose(pst[:, 0:NST], st_[0:NST, ch * 128:(ch + 1) * 128], ident[0:NST, 0:NST])],
                          reads=[B_st, B_ident], writes=[B_pst])
                    p.op("vector", lambda e: e.tensor_copy(stt[:], pst[:, 0:NST]),
                         reads=[B_pst], writes=[B_stt])
                    p.op("vector", lambda e: e.tensor_reduce(m1[:], stt[:], AX.X, ALU.max),
                         reads=[B_stt], writes=[B_m1])
                    p.op("vector", lambda e: e.tensor_single_scalar(mk1[:], stt[:], m1[:, 0:1], ALU.is_equal),
                         reads=[B_stt, B_m1], writes=[B_mk1])
                    p.op("vector", lambda e: e.scalar_tensor_tensor(mk2[:], mk1[:], -1e30, stt[:], ALU.mult, ALU.add),
                         reads=[B_mk1, B_stt], writes=[B_mk2])
                    p.op("vector", lambda e: e.tensor_reduce(m2[:], mk2[:], AX.X, ALU.max),
                         reads=[B_mk2], writes=[B_m2])
                    p.op("vector", lambda e: e.tensor_single_scalar(mk2[:], mk2[:], m2[:, 0:1], ALU.is_equal),
                         reads=[B_m2], writes=[B_mk2])
                    p.op("vector", lambda e: e.tensor_tensor(w1[:], m1[:], m2[:], ALU.subtract),
                         reads=[B_m1, B_m2], writes=[B_w1])
                    p.op("scalar", lambda e: e.activation(w1[:], w1[:], AF.Sigmoid),
                         reads=[B_w1], writes=[B_w1])
                    p.op("vector", lambda e: e.tensor_scalar(w2[:], w1[:], -1.0, 1.0, ALU.mult, ALU.add),
                         reads=[B_w1], writes=[B_w2])
                    p.op("vector", lambda e: e.tensor_scalar_mul(ct[:], mk1[:], w1[:, 0:1]),
                         reads=[B_mk1, B_w1], writes=[B_ct])
                    p.op("vector", lambda e: e.scalar_tensor_tensor(ct[:], mk2[:], w2[:, 0:1], ct[:], ALU.mult, ALU.add),
                         reads=[B_mk2, B_w2], writes=[B_ct])
                    p.mmg([lambda e: e.transpose(cps[0:NST, 0:128], ct[:], ident[:])],
                          reads=[B_ct, B_ident], writes=[B_cps])
                    p.op("vector", lambda e, fd=fd, ch=ch:
                         e.tensor_copy(comb_sb[0:NST, fd * FD + ch * 128:fd * FD + (ch + 1) * 128], cps[0:NST, 0:128]),
                         reads=[B_cps], writes=[B_comb])
            p.barrier()

        if stop_after >= 7:
            do_router()

        def do_experts():
          with p.scope() as sc:
            fb, B_fb = [], []
            for i in range(DBLK):
                t, b = p.sb(sc, f"efb{i}", [128, HTOK], BF16)
                fb.append(t); B_fb.append(b)
                p.dma("gpsimd", t[:], flat_scr[i], reads=[S_flat], writes=[b])
            upsl, B_upsl = p.sb(sc, "upsl", [128, 4 * DBLK * 128], BF16)
            gpsl, B_gpsl = p.sb(sc, "gpsl", [128, 4 * DBLK * 128], BF16)
            dnsl, B_dnsl = p.sb(sc, "dnsl", [128, DBLK * 4 * 128], BF16)

            hid, B_hid = [], []
            for m in range(4):
                t, b = p.sb(sc, f"hid{m}", [128, HTOK], BF16)
                hid.append(t); B_hid.append(b)
            psu, B_psu = [], []
            for i in range(4):
                t, b = p.ps(sc, f"eps{i}", [128, FD])
                psu.append(t); B_psu.append(b)
            sgt, B_sgt = p.sb(sc, "sgt", [128, FD], BF16)
            cbp, B_cbp = p.ps(sc, "cbp", [128, FD])
            ot, B_ot = [], []
            for i in range(2):
                t, b = p.sb(sc, f"eot{i}", [128, FD], F32)
                ot.append(t); B_ot.append(b)

            for k in range(NEXP + 2):
                shared = (k >= NEXP)
                s = k - NEXP
                if shared:
                    p.dma("gpsimd", gpsl[:].rearrange("p (m k j) -> p m k j", m=4, k=DBLK),
                          shg_t[4 * s:4 * s + 4].rearrange("m k p j -> p m k j"), writes=[B_gpsl])
                    p.dma("gpsimd", upsl[:].rearrange("p (m k j) -> p m k j", m=4, k=DBLK),
                          shu_t[4 * s:4 * s + 4].rearrange("m k p j -> p m k j"), writes=[B_upsl])
                    p.dma("gpsimd", dnsl[:].rearrange("p (m k j) -> p m k j", m=DBLK, k=4),
                          shd_t[s].rearrange("m k p j -> p m k j"), writes=[B_dnsl])
                else:
                    p.dma("gpsimd", gpsl[:].rearrange("p (m k j) -> p m k j", m=4, k=DBLK),
                          gp_t[k].rearrange("m k p j -> p m k j"), writes=[B_gpsl])
                    p.dma("gpsimd", upsl[:].rearrange("p (m k j) -> p m k j", m=4, k=DBLK),
                          up_t[k].rearrange("m k p j -> p m k j"), writes=[B_upsl])
                    p.dma("gpsimd", dnsl[:].rearrange("p (m k j) -> p m k j", m=DBLK, k=4),
                          dn_t[k].rearrange("m k p j -> p m k j"), writes=[B_dnsl])

                for m in range(4):
                    for fd in range(NFD_F):
                        pg, Bpg = (psu[0], B_psu[0]) if fd % 2 == 0 else (psu[1], B_psu[1])
                        pu, Bpu = (psu[2], B_psu[2]) if fd % 2 == 0 else (psu[3], B_psu[3])
                        p.mmg([(lambda e, kb=kb, m=m, fd=fd, pg=pg:
                                e.matmul(pg[:, 0:FD], gpsl[:, (m * DBLK + kb) * 128:(m * DBLK + kb + 1) * 128],
                                         fb[kb][:, fd * FD:(fd + 1) * FD],
                                         start=(kb == 0), stop=(kb == DBLK - 1)))
                               for kb in range(DBLK)],
                              reads=[B_gpsl] + B_fb, writes=[Bpg])
                        p.mmg([(lambda e, kb=kb, m=m, fd=fd, pu=pu:
                                e.matmul(pu[:, 0:FD], upsl[:, (m * DBLK + kb) * 128:(m * DBLK + kb + 1) * 128],
                                         fb[kb][:, fd * FD:(fd + 1) * FD],
                                         start=(kb == 0), stop=(kb == DBLK - 1)))
                               for kb in range(DBLK)],
                              reads=[B_upsl] + B_fb, writes=[Bpu])
                        p.op("scalar", lambda e, pg=pg: e.activation(sgt[:], pg[:, 0:FD], AF.Silu),
                             reads=[Bpg], writes=[B_sgt])
                        p.op("vector", lambda e, m=m, fd=fd, pu=pu:
                             e.tensor_tensor(hid[m][:, fd * FD:(fd + 1) * FD], sgt[:], pu[:, 0:FD], ALU.mult),
                             reads=[B_sgt, Bpu], writes=[B_hid[m]])
                if not shared:
                    for fd in range(NFD_F):
                        p.mmg([lambda e, fd=fd, k=k:
                               e.matmul(cbp[:, 0:FD], sel16[:, k * 128:(k + 1) * 128],
                                        comb_sb[:, fd * FD:(fd + 1) * FD],
                                        start=True, stop=True)],
                              reads=[B_sel16, B_comb], writes=[B_cbp])
                        for m in range(4):
                            p.op("vector", lambda e, m=m, fd=fd:
                                 e.tensor_tensor(hid[m][:, fd * FD:(fd + 1) * FD],
                                                 hid[m][:, fd * FD:(fd + 1) * FD], cbp[:, 0:FD], ALU.mult),
                                 reads=[B_cbp], writes=[B_hid[m]])
                for m2 in range(DBLK):
                    for fd in range(NFD_F):
                        po, Bpo = (psu[0], B_psu[0]) if fd % 2 == 0 else (psu[1], B_psu[1])
                        p.mmg([(lambda e, kb=kb, m2=m2, fd=fd, po=po:
                                e.matmul(po[:, 0:FD], dnsl[:, (m2 * 4 + kb) * 128:(m2 * 4 + kb + 1) * 128],
                                         hid[kb][:, fd * FD:(fd + 1) * FD],
                                         start=(kb == 0), stop=(kb == 3)))
                               for kb in range(4)],
                              reads=[B_dnsl] + B_hid, writes=[Bpo])
                        osl = fd % 2
                        p.op("vector", lambda e, po=po, osl=osl: e.tensor_copy(ot[osl][:], po[:, 0:FD]),
                             reads=[Bpo], writes=[B_ot[osl]])
                        if k == 0:
                            p.dma("gpsimd", facc_scr[m2, :, fd * FD:(fd + 1) * FD], ot[osl][:],
                                  reads=[B_ot[osl]], writes=[S_facc])
                        else:
                            p.dma("gpsimd", facc_scr[m2, :, fd * FD:(fd + 1) * FD], ot[osl][:],
                                  reads=[B_ot[osl]], writes=[S_facc], accum_op=ALU.add)
            p.barrier()

        do_experts()

    def do_combine2():
      with p.scope() as sc:
        ft, B_ft = [], []
        for i in range(DBLK):
            t, b = p.sb(sc, f"c2f{i}", [128, FD], F32)
            ft.append(t); B_ft.append(b)
        xct, B_xct2 = p.sb(sc, "c2x", [128, FD], F32)
        ws2 = ln_ws(sc, "c2")
        for fd in range(NFD_F):
            for i in range(DBLK):
                src = facc_scr if ffn else xc_scr
                sbf = S_facc if ffn else S_xc
                p.dma("sync", ft[i][:], src[i, :, fd * FD:(fd + 1) * FD],
                      reads=[sbf], writes=[B_ft[i]])

            def consume3(i, outt, Bout, fd=fd):
                p.dma("sync", xct[:], xc_scr[i, :, fd * FD:(fd + 1) * FD],
                      reads=[S_xc], writes=[B_xct2])
                p.op("vector", lambda e: e.tensor_tensor(xct[:], xct[:], outt[:, 0:FD], ALU.add),
                     reads=[Bout], writes=[B_xct2])
                p.dma("sync", outT[i * 128:(i + 1) * 128, fd * FD:(fd + 1) * FD], xct[:],
                      reads=[B_xct2], writes=[S_out])
            feature_ln(ws2, ft, B_ft, FD, consume3)

    if stop_after >= 9:
        do_combine2()

    for nmm in dumps:
        src = scr_map[nmm]
        shp = list(src.shape)
        dt = src.dtype
        dst = nc.dram_tensor(f"dump_{nmm}", shp, dt, kind="ExternalOutput").ap()
        p.barrier()
        p.dma("sync", dst, src, reads=[scr_buf[nmm]], writes=[S_out])

    p.barrier()

    p.finalize()
    return nc


_CACHE = {}


def kernel(**inputs):
    host = prepare_host_inputs(inputs)
    nc = build()
    htok = 2048
    in_maps = []
    for core in range(8):
        b, h = core // 2, core % 2
        m = {k: v for k, v in host.items() if k != "xT"}
        m["xT"] = host["xT"][b]
        m["tok_base"] = np.full((1, 1), h * htok, np.int32)
        in_maps.append(m)
    res = run_bass_kernel_spmd(nc, in_maps, core_ids=list(range(8)))
    full = np.empty((4, 4096, 1024), np.float32)
    for core in range(8):
        b, h = core // 2, core % 2
        full[b, h * htok:(h + 1) * htok, :] = np.asarray(res.results[core]["outT"]).T
    return full
